# version 07
# speedup vs baseline: 1.4890x; 1.0314x over previous
import math
import numpy as np
import ml_dtypes
from contextlib import ExitStack

import concourse.bass as bass
import concourse.mybir as mybir
from concourse.bass_core_rust import IndirectOffsetOnAxis

F32 = mybir.dt.float32
BF16 = mybir.dt.bfloat16
I32 = mybir.dt.int32
AF = mybir.ActivationFunctionType
ALU = mybir.AluOpType
AXX = mybir.AxisListType.X

SHIFT = 6.0
NEGBIG = -1e9
EPS = 1e-5


class MK:
    def __init__(self, T=4096, dense=False, CE=1024, IC=256, dbg=None):
        self.T = T
        self.S = T // 2
        self.NH = 16
        self.C = 1024
        self.HD = 64
        self.RD = 32
        self.LAT = 256
        self.I = 2048
        self.ER = 7
        self.dense = dense
        self.CE = CE
        self.IC = IC
        self.dbg = dbg or []
        self.phases = 99
        self.nc = bass.Bass("TRN2", target_bir_lowering=False)
        self.es = ExitStack()
        self.scr = {}
        self._phase_sems = []
        self._persist_sems = []

    def sem(self, name, persist=False):
        h = self.nc.alloc_semaphore(name)
        if persist:
            self._persist_sems.append(h)
        else:
            self._phase_sems.append(h)
        return h

    def end_phase(self):
        self.nc.all_engine_barrier()
        if self._phase_sems:
            self.nc.clear_and_free_semaphores(self._phase_sems)
            self._phase_sems = []
            self.nc.all_engine_barrier()

    def sb(self, name, shape, dt=F32, stack=None):
        return (stack or self.es).enter_context(self.nc.sbuf_tensor(name, shape, dt))

    def psum(self, name, shape, dt=F32, stack=None):
        return (stack or self.es).enter_context(self.nc.psum_tensor(name, shape, dt))

    def dram(self, name, shape, dt=F32, kind="Internal"):
        ap = self.nc.dram_tensor(name, shape, dt, kind=kind).ap()
        if kind == "Internal":
            self.scr[name] = ap
        return ap

    def emit_masks(self, blk_vec, blk_gp, masks, cv, ga):
        @blk_vec
        def _(v):
            v.memset(masks[:, 0:512], 0.0)
            v.memset(masks[:, 5 * 512:6 * 512], NEGBIG)
            v.drain()
            v.nop().then_inc(cv, 1)

        @blk_gp
        def _(g):
            g.wait_ge(cv, 1)
            for j in range(4):
                ins = g.affine_select(masks[:, (1 + j) * 512:(2 + j) * 512],
                                      masks[:, 0:512],
                                      [[0, 1], [0, 1], [0, 1], [1, 512]],
                                      ALU.is_ge, NEGBIG,
                                      base=-(j * 128), channel_multiplier=-1)
            ins.then_inc(ga, 1)

    def fp32mm(self, eng, out_ps, lh, ll, rh, rl, start, stop, inc=None, incval=1):
        eng.matmul(out_ps, lh, rh, start=start, stop=False)
        eng.matmul(out_ps, ll, rh, start=False, stop=False)
        ins = eng.matmul(out_ps, lh, rl, start=False, stop=stop)
        if inc is not None:
            ins.then_inc(inc, incval)
        return ins

    def build(self):
        nc, T, S, C, NH = self.nc, self.T, self.S, self.C, self.NH

        self.xin = self.dram("xin", [T, C], F32, kind="ExternalInput")
        self.role_in = self.dram("role_in", [1, 1], I32, kind="ExternalInput")
        self.rolef_in = self.dram("rolef_in", [1, 2], F32, kind="ExternalInput")
        shapes = {
            "w_q": [self.NH, C, self.HD], "w_k": [self.NH, C, self.HD], "w_v": [self.NH, C, self.HD],
            "w_o": [C, C], "ln1_w": [1, C], "ln2_w": [1, C],
            "kv_d": [self.LAT, C], "q_d": [self.LAT, C],
            "k_u": [NH * self.HD, self.LAT], "q_u": [NH * self.HD, self.LAT],
            "v_u": [NH * self.HD, self.LAT],
            "rope_q_w": [NH * self.RD, self.LAT], "rope_k_w": [NH * self.RD, C],
            "o_w": [C, C], "router_w": [self.ER, C], "routing_bias": [1, self.ER],
            "shared_gate": [self.I, C], "shared_up": [self.I, C], "shared_down": [C, self.I],
            "routed_gate": [self.ER, self.I, C], "routed_up": [self.ER, self.I, C],
            "routed_down": [self.ER, C, self.I],
        }
        self.win = {k: self.dram("win_" + k, v, F32, kind="ExternalInput")
                    for k, v in shapes.items()}
        self.out = self.dram("out", [S, C], F32, kind="ExternalOutput")

        d = self.dram
        d("y_hi", [T, C], BF16); d("y_lo", [T, C], BF16)
        d("k_sc", [T, 2 * C], BF16)
        d("q_sc", [T, 2 * C], BF16)
        d("v_sc", [T, 2 * C], BF16)
        d("oA_hi", [C, T], BF16); d("oA_lo", [C, T], BF16)
        d("xmid_sc", [T, C], F32)
        d("y2_sc", [T, C], F32)
        d("u_hi", [T, C], BF16); d("u_lo", [T, C], BF16)
        d("krot_hi", [512, T], BF16); d("krot_lo", [512, T], BF16)
        d("qrot_hi", [512, T], BF16); d("qrot_lo", [512, T], BF16)
        d("qrotO_hi", [512, S], BF16); d("qrotO_lo", [512, S], BF16)
        d("oB_hi", [C, S], BF16); d("oB_lo", [C, S], BF16)
        d("s1_sc", [S, C], F32)
        d("y2O", [S, C], F32); d("xmidO", [S, C], F32); d("z1O", [S, C], F32)
        d("w2t_hi", [S, C], BF16); d("w2t_lo", [S, C], BF16)
        d("moe_acc", [S, C], F32)
        d("wgt_sc", [S, 8], F32)
        if not self.dense:
            NSLOT = self.ER * self.CE + 128
            d("gsc_hi", [NSLOT, C], BF16); d("gsc_lo", [NSLOT, C], BF16)
            d("wgtS_sc", [NSLOT, 1], F32)
            d("posb", [S, 2], I32)
            d("resbuf", [NSLOT, C], F32)
        d("wsthi", [2048, 2048], BF16); d("wstlo", [2048, 2048], BF16)
        d("wsthi2", [2048, 2048], BF16); d("wstlo2", [2048, 2048], BF16)

        self.consts = self.sb("consts", [128, 8], F32)
        self.lnw1B = self.sb("lnw1B", [128, C], F32)
        self.lnw2B = self.sb("lnw2B", [128, C], F32)
        self.biastab = self.sb("biastab", [128, 2], F32)
        self.role_sb = self.sb("role_sb", [1, 1], I32)
        self.rolemB = self.sb("rolemB", [128, 2], F32)

        self.p0_prelude()
        if self.phases >= 1: self.p1_ln()
        if self.phases >= 2: self.p2a_qkv()
        if self.phases >= 3: self.p2b_outer_attn()
        if self.phases >= 4: self.p3_outproj_latents()
        if self.phases >= 5: self.p4_inner_attn()
        if hasattr(self, "es34"):
            self.es34.close()
        if self.phases >= 6: self.p5_z1_router()
        if self.phases >= 6 and not self.dense: self.p5d_sort()
        if self.phases >= 7: self.p6_moe()
        if self.phases >= 8: self.p7_final()
        self.dbg_sb = {n: getattr(self, n) for n in getattr(self, "dbg_sbn", [])}
        self.p_dbg()
        return nc

    def p0_prelude(self):
        nc, T = self.nc, self.T
        ld = self.sem("p0_ld"); cv = self.sem("p0_cv"); ga = self.sem("p0_ga")
        with nc.Block() as blk:
            @blk.sync
            def _(sy):
                sy.dma_start(out=self.role_sb[:], in_=self.role_in).then_inc(ld, 16)
                sy.dma_start(out=self.lnw1B[0:1, :], in_=self.win["ln1_w"]).then_inc(ld, 16)
                sy.dma_start(out=self.lnw2B[0:1, :], in_=self.win["ln2_w"]).then_inc(ld, 16)
                sy.dma_start(out=self.rolemB[0:1, :], in_=self.rolef_in).then_inc(ld, 16)
                sy.wait_ge(ld, 16 * 4)

            @blk.vector
            def _(v):
                cs = self.consts
                v.memset(cs[:, 0:1], -SHIFT)
                v.memset(cs[:, 1:2], 0.0)
                v.memset(cs[:, 2:3], 1.0)
                v.memset(cs[:, 3:4], EPS)
                v.memset(cs[:, 4:5], 1.0 / self.C)
                v.memset(cs[:, 5:6], 0.125)
                v.memset(self.biastab[:, 0:1], -SHIFT)
                v.memset(self.biastab[:, 1:2], NEGBIG)
                v.drain()
                v.nop().then_inc(cv, 1)

        nc.all_engine_barrier()
        bsem = self.sem("p0_bc")
        with nc.Block() as blk2:
            @blk2.sync
            def _(sy):
                p, cnt = 1, 0
                while p < 128:
                    q = min(2 * p, 128)
                    sy.dma_start(out=self.lnw1B[p:q, :], in_=self.lnw1B[0:q - p, :]).then_inc(bsem, 16)
                    sy.dma_start(out=self.lnw2B[p:q, :], in_=self.lnw2B[0:q - p, :]).then_inc(bsem, 16)
                    sy.dma_start(out=self.rolemB[p:q, :], in_=self.rolemB[0:q - p, :]).then_inc(bsem, 16)
                    cnt += 3
                    sy.wait_ge(bsem, 16 * cnt)
                    p = q
        self.end_phase()

    def p3t_trig(self):
        nc, T = self.nc, self.T
        st = ExitStack()
        ld = self.sem("p3t_ld"); cv = self.sem("p3t_cv"); ca = self.sem("p3t_ca")
        ga = self.sem("p3t_ga")
        inv16 = self.sb("inv16", [16, 1], F32, st)
        freq = self.sb("p3t_freq", [16, T], F32, st)
        arg = self.sb("p3t_arg", [16, T], F32, st)
        cos16 = self.sb("cos16", [16, T], F32, st)
        sin16 = self.sb("sin16", [16, T], F32, st)
        rmsk = self.sb("p3t_rmsk", [16, T], F32, st)
        with nc.Block() as blk:
            @blk.sync
            def _(sy):
                sy.wait_ge(ca, 3)
                for h in range(8):
                    sy.dma_start(out=self.cosE[16 * h:16 * h + 16, :], in_=cos16[:]).then_inc(ld, 16)
                    sy.dma_start(out=self.sinE[16 * h:16 * h + 16, :], in_=sin16[:]).then_inc(ld, 16)
                sy.wait_ge(ld, 16 * 16)

            @blk.gpsimd
            def _(g):
                g.iota(inv16[:], [[0, 1], [0, 1], [0, 1], [0, 1]], base=0,
                       channel_multiplier=1, allow_small_or_imprecise_dtypes=True)
                g.iota(freq[:], [[0, 1], [0, 1], [0, 1], [1, T]], base=0,
                       channel_multiplier=0, allow_small_or_imprecise_dtypes=True)
                g.drain()
                g.nop().then_inc(ga, 1)

            @blk.vector
            def _(v):
                msk = arg
                v.wait_ge(ga, 1)
                v.wait_ge(ca, 1)
                v.tensor_scalar_mul(freq[:], freq[:], inv16[:, 0:1])
                v.drain()

                def range_reduce(dst, add0):
                    import numpy as _np
                    v.tensor_scalar_add(dst, freq[:], add0)
                    v.drain()
                    for k in range(9, -2, -1):
                        c = 2.0 * math.pi * (2.0 ** k) if k >= 0 else math.pi
                        sub = 2.0 * math.pi * (2.0 ** k) if k >= 0 else 2.0 * math.pi
                        c_f = float(_np.float32(c))
                        s_hi = float(_np.float32(sub))
                        s_lo = float(_np.float32(sub - float(_np.float64(s_hi))))
                        v.tensor_scalar(rmsk[:], dst, c_f, None, ALU.is_ge)
                        v.drain()
                        v.scalar_tensor_tensor(dst, rmsk[:], -s_hi, dst, ALU.mult, ALU.add)
                        v.drain()
                        v.scalar_tensor_tensor(dst, rmsk[:], -s_lo, dst, ALU.mult, ALU.add)
                        v.drain()

                range_reduce(arg[:], 0.0)
                v.nop().then_inc(cv, 1)
                v.wait_ge(ca, 2)
                range_reduce(arg[:], float(np.float32(math.pi / 2)))
                v.nop().then_inc(cv, 2)

            @blk.scalar
            def _(sc):
                sc.wait_ge(ga, 1)
                sc.activation(inv16[:], inv16[:], AF.Exp,
                              bias=self.consts[0:16, 1:2],
                              scale=-math.log(10000.0) / 16.0).then_inc(ca, 1)
                sc.wait_ge(cv, 1)
                sc.activation(sin16[:], arg[:], AF.Sin,
                              bias=self.consts[0:16, 1:2], scale=1.0).then_inc(ca, 1)
                sc.wait_ge(cv, 3)
                sc.activation(cos16[:], arg[:], AF.Sin,
                              bias=self.consts[0:16, 1:2], scale=1.0).then_inc(ca, 1)

        self.end_phase()
        st.close()

    def ln_tile(self, v, sc, x_ap, sq, s, y32, yhi, ylo, wB=None,
                sv=None, sa=None, i=0, spt=4, apt=2):
        raise NotImplementedError

    def p1_ln(self):
        self._ln_stream(src_dram=self.xin, dst_hi=self.scr["y_hi"], dst_lo=self.scr["y_lo"],
                        n_rows=self.T, wB=None, dst32=None, tag="p1")

    def _ln_stream(self, src_dram, dst_hi, dst_lo, n_rows, wB=None, dst32=None,
                   tag="ln"):
        nc, C = self.nc, self.C
        n_tt = n_rows // 128
        st = ExitStack()
        NB = 3
        xt = [self.sb(f"{tag}_x{i}", [128, C], F32, st) for i in range(NB)]
        sq = [self.sb(f"{tag}_sq{i}", [128, C], F32, st) for i in range(NB)]
        yhi = [self.sb(f"{tag}_yhi{i}", [128, C], BF16, st) for i in range(NB)]
        ylo = [self.sb(f"{tag}_ylo{i}", [128, C], BF16, st) for i in range(NB)]
        y32 = [self.sb(f"{tag}_y32{i}", [128, C], F32, st) for i in range(NB)]
        stats = [self.sb(f"{tag}_st{i}", [128, 8], F32, st) for i in range(NB)]
        ld = [self.sem(f"{tag}_ld{i}") for i in range(NB)]
        so = [self.sem(f"{tag}_so{i}") for i in range(NB)]
        sv = self.sem(tag + "_sv")
        sa = self.sem(tag + "_sa")
        npt = 2 + (1 if dst32 is not None else 0)
        with nc.Block() as blk:
            @blk.sync
            def _(sy):
                for i in range(n_tt):
                    b = i % NB
                    if i >= NB:
                        sy.wait_ge(sv, 3 * (i - NB) + 3)
                        sy.wait_ge(sa, 3 * (i - NB) + 3)
                    sy.dma_start(out=xt[b][:],
                                 in_=src_dram[i * 128:(i + 1) * 128, :]).then_inc(ld[b], 16)

            @blk.vector
            def _(v):
                for i in range(n_tt):
                    b = i % NB
                    s = stats[b]
                    if i >= NB:
                        v.wait_ge(so[b], 16 * npt * (i // NB))
                    v.wait_ge(ld[b], 16 * (i // NB + 1))
                    v.reduce_sum(s[:, 0:1], xt[b][:], axis=AXX)
                    v.wait_ge(sa, 3 * i + 1)
                    v.reduce_sum(s[:, 1:2], sq[b][:], axis=AXX)
                    v.drain()
                    v.tensor_scalar_mul(s[:, 2:3], s[:, 0:1], self.consts[:, 4:5])
                    v.drain()
                    v.tensor_scalar(s[:, 3:4], s[:, 2:3], s[:, 2:3], None, ALU.mult)
                    v.tensor_scalar_mul(s[:, 1:2], s[:, 1:2], self.consts[:, 4:5])
                    v.drain()
                    v.tensor_sub(s[:, 3:4], s[:, 1:2], s[:, 3:4])
                    v.drain()
                    v.tensor_scalar_add(s[:, 3:4], s[:, 3:4], EPS).then_inc(sv, 1)
                    v.wait_ge(sa, 3 * i + 2)
                    v.reciprocal(s[:, 5:6], s[:, 4:5])
                    v.drain()
                    v.tensor_scalar(y32[b][:], xt[b][:], s[:, 2:3], s[:, 5:6],
                                    ALU.subtract, ALU.mult)
                    v.drain()
                    if wB is not None:
                        v.tensor_mul(y32[b][:], y32[b][:], wB[:])
                        v.drain()
                    v.nop().then_inc(sv, 1)
                    v.wait_ge(sa, 3 * i + 3)
                    v.tensor_sub(ylo[b][:], y32[b][:], yhi[b][:])
                    v.drain()
                    v.nop().then_inc(sv, 1)

            @blk.scalar
            def _(sc):
                for i in range(n_tt):
                    b = i % NB
                    s = stats[b]
                    if i >= NB:
                        sc.wait_ge(so[b], 16 * npt * (i // NB))
                    sc.wait_ge(ld[b], 16 * (i // NB + 1))
                    sc.activation(sq[b][:], xt[b][:], AF.Square,
                                  bias=self.consts[:, 1:2], scale=1.0).then_inc(sa, 1)
                    sc.wait_ge(sv, 3 * i + 1)
                    sc.activation(s[:, 4:5], s[:, 3:4], AF.Sqrt,
                                  bias=self.consts[:, 1:2], scale=1.0).then_inc(sa, 1)
                    sc.wait_ge(sv, 3 * i + 2)
                    sc.copy(yhi[b][:], y32[b][:]).then_inc(sa, 1)
                    sc.wait_ge(sa, 3 * i + 3)
                    sc.dma_start(out=dst_hi[i * 128:(i + 1) * 128, :],
                                 in_=yhi[b][:]).then_inc(so[b], 16)
                    if dst32 is not None:
                        sc.dma_start(out=dst32[i * 128:(i + 1) * 128, :],
                                     in_=y32[b][:]).then_inc(so[b], 16)
                    sc.wait_ge(sv, 3 * i + 3)
                    sc.dma_start(out=dst_lo[i * 128:(i + 1) * 128, :],
                                 in_=ylo[b][:]).then_inc(so[b], 16)
                for b in range(NB):
                    nfull = len([i for i in range(n_tt) if i % NB == b])
                    sc.wait_ge(so[b], 16 * npt * nfull)
        self.end_phase()
        st.close()

    def p2a_qkv(self):
        nc, T, C, NH = self.nc, self.T, self.C, self.NH
        n_tg = T // 512
        st = ExitStack()
        wq_hi = self.sb("p2_wq_hi", [128, 8 * C], BF16, st)
        wq_lo = self.sb("p2_wq_lo", [128, 8 * C], BF16, st)
        wk_hi = self.sb("p2_wk_hi", [128, 8 * C], BF16, st)
        wk_lo = self.sb("p2_wk_lo", [128, 8 * C], BF16, st)
        wv_hi = self.sb("p2_wv_hi", [128, 8 * C], BF16, st)
        wv_lo = self.sb("p2_wv_lo", [128, 8 * C], BF16, st)
        wstage = self.sb("p2_wstage", [128, C], F32, st)
        ldw = self.sem("p2_ldw")
        cvw = self.sem("p2_cvw")
        NB = 2
        yf_hi = [self.sb(f"p2_yfh{i}", [128, 8 * 512], BF16, st) for i in range(NB)]
        yf_lo = [self.sb(f"p2_yfl{i}", [128, 8 * 512], BF16, st) for i in range(NB)]
        stg = [self.sb(f"p2_stg{i}", [128, 2 * C], BF16, st) for i in range(NB * 3)]
        pshi = self.psum("p2_ps0", [128, 512], F32, st)
        pslo = self.psum("p2_ps1", [128, 512], F32, st)
        ldy = [self.sem(f"p2_ldy{i}") for i in range(NB)]
        mms = self.sem("p2_mm")
        evs = self.sem("p2_ev")
        sos = [self.sem(f"p2_so{i}") for i in range(NB * 3)]
        W = [("q", wq_hi, wq_lo, self.win["w_q"], self.scr["q_sc"]),
             ("k", wk_hi, wk_lo, self.win["w_k"], self.scr["k_sc"]),
             ("v", wv_hi, wv_lo, self.win["w_v"], self.scr["v_sc"])]
        with nc.Block() as blk:
            @blk.sync
            def _(sy):
                for wi in range(3):
                    for cchunk in range(8):
                        ci = 8 * wi + cchunk
                        if ci >= 1:
                            sy.wait_ge(cvw, ci)
                        for h in range(NH):
                            sy.dma_start(
                                out=wstage[:, h * 64:h * 64 + 64],
                                in_=W[wi][3][h, cchunk * 128:(cchunk + 1) * 128, :],
                            ).then_inc(ldw, 16)
                for g in range(n_tg):
                    b = g % NB
                    if g >= NB:
                        sy.wait_ge(mms, 12 * (g - NB + 1))
                    for cchunk in range(8):
                        sy.dma_start(out=yf_hi[b][:, cchunk * 512:(cchunk + 1) * 512],
                                     in_=self.scr["y_hi"][g * 512:(g + 1) * 512,
                                                          cchunk * 128:(cchunk + 1) * 128],
                                     transpose=True).then_inc(ldy[b], 16)
                        sy.dma_start(out=yf_lo[b][:, cchunk * 512:(cchunk + 1) * 512],
                                     in_=self.scr["y_lo"][g * 512:(g + 1) * 512,
                                                          cchunk * 128:(cchunk + 1) * 128],
                                     transpose=True).then_inc(ldy[b], 16)

            @blk.vector
            def _(v):
                for wi in range(3):
                    whi, wlo = W[wi][1], W[wi][2]
                    for cchunk in range(8):
                        v.wait_ge(ldw, 16 * NH * (8 * wi + cchunk + 1))
                        v.tensor_copy(whi[:, cchunk * C:(cchunk + 1) * C], wstage[:])
                        v.drain()
                        v.tensor_sub(wlo[:, cchunk * C:(cchunk + 1) * C], wstage[:], whi[:, cchunk * C:(cchunk + 1) * C])
                        v.drain()
                        v.nop().then_inc(cvw, 1)
                for g in range(n_tg):
                    for wi in range(3):
                        sb_i = (g % NB) * 3 + wi
                        sgb = stg[sb_i]
                        for sub in range(4):
                            idx = 12 * g + 4 * wi + sub
                            v.wait_ge(mms, idx + 1)
                            if idx >= 1:
                                v.wait_ge(sos[sb_i], 16 * (4 * (g // NB) + sub))
                            dhi = sgb[:].rearrange("p (h two s) -> p h two s", two=2, s=64)
                            v.tensor_copy(dhi[:, 0:8, 0, :], pshi[:].rearrange("p (h s) -> p h s", s=64))
                            v.tensor_copy(dhi[:, 8:16, 0, :], pslo[:].rearrange("p (h s) -> p h s", s=64))
                            v.drain()
                            v.tensor_tensor(dhi[:, 0:8, 1, :],
                                            pshi[:].rearrange("p (h s) -> p h s", s=64),
                                            dhi[:, 0:8, 0, :], ALU.subtract)
                            v.tensor_tensor(dhi[:, 8:16, 1, :],
                                            pslo[:].rearrange("p (h s) -> p h s", s=64),
                                            dhi[:, 8:16, 0, :], ALU.subtract)
                            v.drain()
                            v.nop().then_inc(evs, 1)

            @blk.tensor
            def _(t):
                t.wait_ge(cvw, 24)
                for g in range(n_tg):
                    b = g % NB
                    t.wait_ge(ldy[b], 16 * 16 * (g // NB + 1))
                    for wi in range(3):
                        whi, wlo = W[wi][1], W[wi][2]
                        for sub in range(4):
                            idx = 12 * g + 4 * wi + sub
                            if idx >= 1:
                                t.wait_ge(evs, idx)
                            for half, ps in ((0, pshi), (1, pslo)):
                                for cchunk in range(8):
                                    lh = yf_hi[b][:, cchunk * 512 + sub * 128: cchunk * 512 + sub * 128 + 128]
                                    ll = yf_lo[b][:, cchunk * 512 + sub * 128: cchunk * 512 + sub * 128 + 128]
                                    rh = whi[:, cchunk * C + half * 512: cchunk * C + half * 512 + 512]
                                    rl = wlo[:, cchunk * C + half * 512: cchunk * C + half * 512 + 512]
                                    last = (cchunk == 7 and half == 1)
                                    self.fp32mm(t, ps[:], lh, ll, rh, rl,
                                                start=(cchunk == 0), stop=(cchunk == 7),
                                                inc=mms if last else None)

            @blk.scalar
            def _(sc):
                for g in range(n_tg):
                    for wi in range(3):
                        sb_i = (g % NB) * 3 + wi
                        sgb = stg[sb_i]
                        dst = W[wi][4]
                        for sub in range(4):
                            idx = 12 * g + 4 * wi + sub
                            sc.wait_ge(evs, idx + 1)
                            r0 = g * 512 + sub * 128
                            sc.dma_start(out=dst[r0:r0 + 128, :], in_=sgb[:]).then_inc(sos[sb_i], 16)
                for sb_i in range(NB * 3):
                    ng = len([g for g in range(n_tg) if (g % NB) == sb_i // 3])
                    sc.wait_ge(sos[sb_i], 16 * 4 * ng)
        self.end_phase()
        st.close()

    def p2b_outer_attn(self):
        self._attn_dst_hi = self.scr["oA_hi"]
        self._attn_dst_lo = self.scr["oA_lo"]
        self._attn(tag="o", q_dram_hi=None)

    def _attn(self, tag, q_dram_hi):
        nc, T, C, NH = self.nc, self.T, self.C, self.NH
        n_tg = T // 512
        st = ExitStack()
        self.masks = self.sb(f"{tag}_masks", [128, 6 * 512], BF16, st)
        mcv = self.sem(f"{tag}_mcv"); mga = self.sem(f"{tag}_mga")
        with nc.Block() as mblk:
            self.emit_masks(mblk.vector, mblk.gpsimd, self.masks, mcv, mga)
        nc.all_engine_barrier()
        NB = 2
        kT = [self.sb(f"{tag}a_kT{i}", [128, T], BF16, st) for i in range(NB)]
        qT = [self.sb(f"{tag}a_qT{i}", [128, T], BF16, st) for i in range(NB)]
        vI = [self.sb(f"{tag}a_v{i}", [128, T], BF16, st) for i in range(NB)]
        q1 = [self.sb(f"{tag}a_q1{i}", [128, T], BF16, st) for i in range(NB)]
        q2 = [self.sb(f"{tag}a_q2{i}", [64, T], BF16, st) for i in range(NB)]
        NP = 3
        p32 = [self.sb(f"{tag}a_p32{i}", [128, 512], F32, st) for i in range(NP)]
        ph = [self.sb(f"{tag}a_ph{i}", [128, 512], BF16, st) for i in range(NP)]
        pl = [self.sb(f"{tag}a_pl{i}", [128, 512], BF16, st) for i in range(NP)]
        ones128 = self.sb(f"{tag}a_ones128", [128, 1], BF16, st)
        ones1 = self.sb(f"{tag}a_ones1", [1, 64], F32, st)
        ost = [self.sb(f"{tag}a_ost{i}", [128, 512], BF16, st) for i in range(2)]
        o32 = [self.sb(f"{tag}a_o32{i}", [64, 512], F32, st) for i in range(2)]
        rlt = [self.sb(f"{tag}a_rl{i}", [1, 512], F32, st) for i in range(2)]
        s_ps = [self.psum(f"{tag}a_sps{i}", [128, 512], F32, st) for i in range(2)]
        oA_ps = [self.psum(f"{tag}a_oA{i}", [128, 512], F32, st) for i in range(2)]
        oB_ps = [self.psum(f"{tag}a_oB{i}", [64, 512], F32, st) for i in range(2)]
        l_ps = self.psum(f"{tag}a_l", [1, 512], F32, st)
        rl_ps = self.psum(f"{tag}a_rlb", [64, 512], F32, st)
        ldh = [self.sem(f"{tag}a_ld{i}") for i in range(NB)]
        sS = self.sem(f"{tag}a_sS")
        sM = self.sem(f"{tag}a_sM")
        sE = self.sem(f"{tag}a_sE")
        sP = self.sem(f"{tag}a_sP")
        sA = self.sem(f"{tag}a_sA")
        sG = self.sem(f"{tag}a_sG")
        sR = self.sem(f"{tag}a_sR")
        sL = self.sem(f"{tag}a_sL")
        soh = [self.sem(f"{tag}a_so{i}") for i in range(2)]
        ktph = sum(4 * (g + 1) for g in range(n_tg))
        nG = NH * n_tg
        qg_pl = 1 if n_tg > 2 else 0
        q1_ord = {0: 1}
        mask_ord = {}
        sMc = 1
        kk = 0
        for h in range(NH):
            for g in range(n_tg):
                for kt in range(4 * (g + 1)):
                    j = kt * 128 - g * 512
                    if 0 <= j:
                        sMc += 1
                        mask_ord[kk] = sMc
                    kk += 1
                if g == qg_pl and h + 1 < NH:
                    sMc += 1
                    q1_ord[h + 1] = sMc
        cumA = []
        acc = 0
        for h in range(NH):
            for g in range(n_tg):
                acc += 4 * (g + 1)
                cumA.append(acc)

        with nc.Block() as blk:
            @blk.sync
            def _(sy):
                for h in range(NH):
                    b = h % NB
                    if h >= NB:
                        sy.wait_ge(sG, 2 * n_tg * (h - NB + 1))
                    sy.dma_start(out=kT[b][:], in_=self.scr["k_sc"][0:T, 128 * h:128 * h + 128],
                                 transpose=True).then_inc(ldh[b], 16)
                    sy.dma_start(out=qT[b][:], in_=self.scr["q_sc"][0:T, 128 * h:128 * h + 128],
                                 transpose=True).then_inc(ldh[b], 16)
                    sy.dma_start(out=vI[b][:].rearrange("p (a c) -> p a c", c=128),
                                 in_=self.scr["v_sc"][0:T, 128 * h:128 * h + 128]
                                 .rearrange("(a p) c -> p a c", p=128)).then_inc(ldh[b], 16)

            @blk.vector
            def _(v):
                v.memset(ones128[:], 1.0)
                v.memset(ones1[:], 1.0)
                v.drain()

                def q1b(h):
                    b = h % NB
                    v.wait_ge(ldh[b], 48 * (h // NB + 1))
                    v.tensor_copy(q1[b][0:64, :], qT[b][0:64, :])
                    v.tensor_copy(q1[b][64:128, :], qT[b][0:64, :])
                    v.tensor_copy(q2[b][:], qT[b][64:128, :]).then_inc(sM, 1)
                q1b(0)
                k = 0
                iG = 0
                for h in range(NH):
                    for g in range(n_tg):
                        nkt = 4 * (g + 1)
                        pg = iG % 2
                        for kt in range(nkt):
                            pi = k % NP
                            j = kt * 128 - g * 512
                            if 0 <= j:
                                v.wait_ge(sS, k + 1)
                                v.tensor_tensor(s_ps[k % 2][:], s_ps[k % 2][:],
                                                self.masks[:, (1 + j // 128) * 512:(2 + j // 128) * 512],
                                                ALU.add).then_inc(sM, 1)
                            v.wait_ge(sE, 2 * k + 2)
                            if k >= NP:
                                v.wait_ge(sA, k - NP + 1)
                            v.tensor_sub(pl[pi][:], p32[pi][:], ph[pi][:]).then_inc(sP, 1)
                            k += 1
                        v.wait_ge(sA, cumA[iG])
                        gb = iG % 2
                        v.wait_ge(soh[gb], 32 * (iG // 2))
                        v.tensor_copy(o32[pg][:], oA_ps[pg][0:64, :])
                        v.tensor_copy(rlt[pg][:], l_ps[0:1, :])
                        v.drain()
                        v.tensor_add(o32[pg][:], o32[pg][:], oA_ps[pg][64:128, :])
                        v.reciprocal(rlt[pg][:], rlt[pg][:]).then_inc(sL, 1)
                        v.drain()
                        v.tensor_add(o32[pg][:], o32[pg][:], oB_ps[pg][0:64, :])
                        v.drain()
                        v.wait_ge(sR, iG + 1)
                        v.tensor_mul(o32[pg][:], o32[pg][:], rl_ps[0:64, :])
                        v.drain()
                        v.tensor_copy(ost[gb][0:64, :], o32[pg][:])
                        v.drain()
                        v.tensor_tensor(ost[gb][64:128, :], o32[pg][:], ost[gb][0:64, :],
                                        ALU.subtract).then_inc(sG, 2)
                        iG += 1
                        if g == qg_pl and h + 1 < NH:
                            q1b(h + 1)

            @blk.tensor
            def _(t):
                k = [0]
                iG = 0
                for h in range(NH):
                    b = h % NB
                    t.wait_ge(ldh[b], 48 * (h // NB + 1))
                    t.wait_ge(sM, q1_ord[h])
                    for g in range(n_tg):
                        nkt = 4 * (g + 1)
                        pg = iG % 2
                        k0 = k[0]

                        def emS(j):
                            kk = k0 + j
                            if kk >= 2:
                                t.wait_ge(sE, 2 * (kk - 1))
                            t.matmul(s_ps[kk % 2][:], kT[b][:, j * 128:(j + 1) * 128],
                                     q1[b][:, g * 512:(g + 1) * 512],
                                     start=True, stop=False)
                            t.matmul(s_ps[kk % 2][:], kT[b][0:64, j * 128:(j + 1) * 128],
                                     q2[b][:, g * 512:(g + 1) * 512],
                                     start=False, stop=True).then_inc(sS, 1)

                        def emA(j):
                            kk = k0 + j
                            t.wait_ge(sP, kk + 1)
                            if j == 0:
                                if iG >= 2:
                                    t.wait_ge(sG, 2 * (iG - 1))
                                if iG >= 1:
                                    t.wait_ge(sL, iG)
                            vsl = vI[b][:, j * 128:(j + 1) * 128]
                            pi = kk % NP
                            t.matmul(oA_ps[pg][:], vsl, ph[pi][:],
                                     start=(j == 0), stop=(j == nkt - 1))
                            t.matmul(oB_ps[pg][:], vsl[:, 0:64], pl[pi][:],
                                     start=(j == 0), stop=(j == nkt - 1))
                            t.matmul(l_ps[0:1, :], ones128[:], ph[pi][:],
                                     start=(j == 0), stop=False)
                            t.matmul(l_ps[0:1, :], ones128[:], pl[pi][:],
                                     start=False, stop=(j == nkt - 1)).then_inc(sA, 1)
                        emS(0); emS(1)
                        if g >= 1:
                            t.wait_ge(sL, iG)
                            t.matmul(rl_ps[:], ones1[:], rlt[(iG - 1) % 2][:], start=True,
                                     stop=True).then_inc(sR, 1)
                        for j in range(nkt):
                            emA(j)
                            if j + 2 < nkt:
                                emS(j + 2)
                        k[0] += nkt
                        if g == n_tg - 1:
                            t.wait_ge(sL, iG + 1)
                            t.matmul(rl_ps[:], ones1[:], rlt[iG % 2][:], start=True,
                                     stop=True).then_inc(sR, 1)
                        iG += 1

            @blk.scalar
            def _(sc):
                k = 0
                for h in range(NH):
                    for g in range(n_tg):
                        nkt = 4 * (g + 1)
                        for kt in range(nkt):
                            j = kt * 128 - g * 512
                            sc.wait_ge(sS, k + 1)
                            if 0 <= j:
                                sc.wait_ge(sM, mask_ord[k])
                            pi = k % NP
                            if k >= NP:
                                sc.wait_ge(sP, k - NP + 1)
                                sc.wait_ge(sA, k - NP + 1)
                            sc.activation(p32[pi][:], s_ps[k % 2][:], AF.Exp,
                                          bias=self.consts[:, 0:1],
                                          scale=self.consts[:, 5:6]).then_inc(sE, 1)
                            sc.activation(ph[pi][:], s_ps[k % 2][:], AF.Exp,
                                          bias=self.consts[:, 0:1],
                                          scale=self.consts[:, 5:6]).then_inc(sE, 1)
                            k += 1

            @blk.gpsimd
            def _(g_):
                iG = 0
                for h in range(NH):
                    for g in range(n_tg):
                        gb = iG % 2
                        g_.wait_ge(sG, 2 * iG + 2)
                        g_.dma_start(out=self._attn_dst_hi[64 * h:64 * h + 64,
                                                           g * 512:(g + 1) * 512],
                                     in_=ost[gb][0:64, :]).then_inc(soh[gb], 16)
                        g_.dma_start(out=self._attn_dst_lo[64 * h:64 * h + 64,
                                                           g * 512:(g + 1) * 512],
                                     in_=ost[gb][64:128, :]).then_inc(soh[gb], 16)
                        iG += 1
                for p in range(2):
                    nuse = len([i for i in range(NH * n_tg) if i % 2 == p])
                    g_.wait_ge(soh[p], 32 * nuse)
        self.end_phase()
        st.close()

    def p3_outproj_latents(self):
        sub = getattr(self, "sub3", 9)
        self.p3a_outproj()
        if sub < 1:
            return
        self.es34 = ExitStack()
        self.kv_lat_hi = self.sb("kv_lat_hi", [128, 2 * self.T], BF16, self.es34)
        self.kv_lat_lo = self.sb("kv_lat_lo", [128, 2 * self.T], BF16, self.es34)
        self.qlo_hi = self.sb("qlow_hi", [128, 2 * self.S], BF16, self.es34)
        self.qlo_lo = self.sb("qlow_lo", [128, 2 * self.S], BF16, self.es34)
        self.es3o = ExitStack()
        self.q_lat_hi = self.sb("q_lat_hi", [128, 2 * self.T], BF16, self.es3o)
        self.q_lat_lo = self.sb("q_lat_lo", [128, 2 * self.T], BF16, self.es3o)
        self.cosE = self.sb("cosE", [128, self.T], F32, self.es3o)
        self.sinE = self.sb("sinE", [128, self.T], F32, self.es3o)
        self._ln_stream(src_dram=self.scr["xmid_sc"], dst_hi=self.scr["y_hi"],
                        dst_lo=self.scr["y_lo"], n_rows=self.T, wB=None,
                        dst32=self.scr["y2_sc"], tag="p3b1")
        if sub >= 2:
            self._ln_stream(src_dram=self.scr["y2_sc"], dst_hi=self.scr["u_hi"],
                            dst_lo=self.scr["u_lo"], n_rows=self.T, wB=self.lnw1B,
                            tag="p3b2")
        if sub >= 3:
            self.p3t_trig()
        if sub >= 4:
            self.p3c_latents()
        if sub >= 5:
            self.p3d_qlo()
        self.es3o.close()

    def p3d_qlo(self):
        nc, T, S = self.nc, self.T, self.S
        with nc.Block() as blk:
            @blk.vector
            def _(v):
                for src_t, dst in ((self.q_lat_hi, self.qlo_hi), (self.q_lat_lo, self.qlo_lo)):
                    for lc in range(2):
                        v.tensor_scalar_mul(dst[:, lc * S:(lc + 1) * S],
                                            src_t[:, lc * T:lc * T + S], self.rolemB[:, 0:1])
                        v.drain()
                        v.scalar_tensor_tensor(dst[:, lc * S:(lc + 1) * S],
                                               src_t[:, lc * T + S:lc * T + 2 * S],
                                               self.rolemB[:, 1:2],
                                               dst[:, lc * S:(lc + 1) * S],
                                               ALU.mult, ALU.add)
                        v.drain()
        self.end_phase()

    def p3a_outproj(self):
        nc, T, C = self.nc, self.T, self.C
        n_tt = T // 128
        st = ExitStack()
        wo_hi = self.sb("p3_wo_hi", [128, 8 * C], BF16, st)
        wo_lo = self.sb("p3_wo_lo", [128, 8 * C], BF16, st)
        wstage = self.sb("p3_wstage", [128, C], F32, st)
        ldw = self.sem("p3_ldw"); cvw = self.sem("p3_cvw")
        NB = 2
        oT = [self.sb(f"p3_oT{i}", [128, 16 * 128], BF16, st) for i in range(NB)]
        xt = [self.sb(f"p3_xt{i}", [128, C], F32, st) for i in range(NB)]
        xm = [self.sb(f"p3_xm{i}", [128, C], F32, st) for i in range(NB)]
        op_ps = [self.psum(f"p3_ps{i}", [128, 512], F32, st) for i in range(2)]
        ldo = [self.sem(f"p3_ldo{i}") for i in range(NB)]
        ldx = [self.sem(f"p3_ldx{i}") for i in range(NB)]
        mms = self.sem("p3_mm")
        evs = self.sem("p3_ev")
        sos = [self.sem(f"p3_so{i}") for i in range(NB)]
        with nc.Block() as blk:
            @blk.sync
            def _(sy):
                for cchunk in range(8):
                    if cchunk >= 1:
                        sy.wait_ge(cvw, cchunk)
                    sy.dma_start(out=wstage[:],
                                 in_=self.win["w_o"][cchunk * 128:(cchunk + 1) * 128, :]
                                 ).then_inc(ldw, 16)
                for i in range(n_tt):
                    b = i % NB
                    if i >= NB:
                        sy.wait_ge(mms, 2 * (i - NB + 1))
                    for cchunk in range(8):
                        sy.dma_start(out=oT[b][:, cchunk * 128:(cchunk + 1) * 128],
                                     in_=self.scr["oA_hi"][cchunk * 128:(cchunk + 1) * 128,
                                                           i * 128:(i + 1) * 128]).then_inc(ldo[b], 16)
                        sy.dma_start(out=oT[b][:, 1024 + cchunk * 128:1024 + (cchunk + 1) * 128],
                                     in_=self.scr["oA_lo"][cchunk * 128:(cchunk + 1) * 128,
                                                           i * 128:(i + 1) * 128]).then_inc(ldo[b], 16)
                    sy.wait_ge(evs, max(0, (i - NB + 1)))
                    sy.dma_start(out=xt[b][:],
                                 in_=self.xin[i * 128:(i + 1) * 128, :]).then_inc(ldx[b], 16)

            @blk.vector
            def _(v):
                for cchunk in range(8):
                    v.wait_ge(ldw, 16 * (cchunk + 1))
                    v.tensor_copy(wo_hi[:, cchunk * C:(cchunk + 1) * C], wstage[:])
                    v.drain()
                    v.tensor_sub(wo_lo[:, cchunk * C:(cchunk + 1) * C], wstage[:],
                                 wo_hi[:, cchunk * C:(cchunk + 1) * C])
                    v.drain()
                    v.nop().then_inc(cvw, 1)
                for i in range(n_tt):
                    b = i % NB
                    if i >= NB:
                        v.wait_ge(sos[b], 16 * (i // NB))
                    v.wait_ge(mms, 2 * i + 1)
                    v.wait_ge(ldx[b], 16 * (i // NB + 1))
                    v.tensor_add(xm[b][:, 0:512], op_ps[0][:], xt[b][:, 0:512])
                    v.wait_ge(mms, 2 * i + 2)
                    v.tensor_add(xm[b][:, 512:1024], op_ps[1][:], xt[b][:, 512:1024])
                    v.drain()
                    v.nop().then_inc(evs, 1)

            @blk.tensor
            def _(t):
                t.wait_ge(cvw, 8)
                for i in range(n_tt):
                    b = i % NB
                    t.wait_ge(ldo[b], 16 * 16 * (i // NB + 1))
                    if i >= 1:
                        t.wait_ge(evs, i)
                    for half in range(2):
                        for cchunk in range(8):
                            lh = oT[b][:, cchunk * 128:(cchunk + 1) * 128]
                            ll = oT[b][:, 1024 + cchunk * 128:1024 + (cchunk + 1) * 128]
                            rh = wo_hi[:, cchunk * C + half * 512:cchunk * C + half * 512 + 512]
                            rl = wo_lo[:, cchunk * C + half * 512:cchunk * C + half * 512 + 512]
                            self.fp32mm(t, op_ps[half][:], lh, ll, rh, rl,
                                        start=(cchunk == 0), stop=(cchunk == 7),
                                        inc=mms if cchunk == 7 else None)

            @blk.scalar
            def _(sc):
                for i in range(n_tt):
                    b = i % NB
                    sc.wait_ge(evs, i + 1)
                    sc.dma_start(out=self.scr["xmid_sc"][i * 128:(i + 1) * 128, :],
                                 in_=xm[b][:]).then_inc(sos[b], 16)
                for b in range(NB):
                    nfull = len([i for i in range(n_tt) if i % NB == b])
                    sc.wait_ge(sos[b], 16 * nfull)
        self.end_phase()
        st.close()

    def p3c_latents(self):
        nc, T, C = self.nc, self.T, self.C
        n_tg = T // 512
        st = ExitStack()
        wstage = self.sb("p3c_wstage", [128, C], F32, st)
        wsplit_hi = self.sb("p3c_wsp_hi", [128, C], BF16, st)
        wsplit_lo = self.sb("p3c_wsp_lo", [128, C], BF16, st)
        kvdT_hi = self.sb("p3c_kvdT_hi", [128, 8 * 256], BF16, st)
        kvdT_lo = self.sb("p3c_kvdT_lo", [128, 8 * 256], BF16, st)
        qdT_hi = self.sb("p3c_qdT_hi", [128, 8 * 256], BF16, st)
        qdT_lo = self.sb("p3c_qdT_lo", [128, 8 * 256], BF16, st)
        rkT_hi = self.sb("p3c_rkT_hi", [128, 8 * 512], BF16, st)
        rkT_lo = self.sb("p3c_rkT_lo", [128, 8 * 512], BF16, st)
        rqT_hi = self.sb("p3c_rqT_hi", [128, 2 * 512], BF16, st)
        rqT_lo = self.sb("p3c_rqT_lo", [128, 2 * 512], BF16, st)
        ldw = self.sem("p3c_ldw")
        cvw = self.sem("p3c_spl")
        stg4 = self.sem("p3c_stg")
        trl = self.sem("p3c_trl")
        sws = self.sem("p3c_sws")
        ldt = self.sem("p3c_ldt")
        NB = 1
        uf_hi = [self.sb(f"p3c_ufh{i}", [128, 8 * 512], BF16, st) for i in range(NB)]
        uf_lo = [self.sb(f"p3c_ufl{i}", [128, 8 * 512], BF16, st) for i in range(NB)]
        ldu = [self.sem(f"p3c_ldu{i}") for i in range(NB)]
        lat_ps = [self.psum(f"p3c_lps{i}", [128, 512], F32, st) for i in range(2)]
        rot_ps = [self.psum(f"p3c_rps{i}", [128, 512], F32, st) for i in range(4)]
        rot32 = [self.sb(f"p3c_r32{i}", [128, 512], F32, st) for i in range(4)]
        rt32 = [self.sb(f"p3c_rt32{i}", [128, 512], F32, st) for i in range(2)]
        rot_hi = self.sb("p3c_rot_hi", [128, 4 * 512], BF16, st)
        rot_lo = self.sb("p3c_rot_lo", [128, 4 * 512], BF16, st)
        mms = self.sem("p3c_mm")
        evs = self.sem("p3c_ev")
        rms = self.sem("p3c_rm")
        rvs = self.sem("p3c_rv")
        ros = [self.sem(f"p3c_ro{i}") for i in range(2)]
        W4 = [("kv_d", kvdT_hi, kvdT_lo, 256, 2), ("q_d", qdT_hi, qdT_lo, 256, 2),
              ("rope_k_w", rkT_hi, rkT_lo, 512, 4), ("rope_q_w", rqT_hi, rqT_lo, 512, 4)]
        with nc.Block() as blk:
            @blk.sync
            def _(sy):
                rid = 0
                for wi, (nm, thi, tlo, nrows, ntile) in enumerate(W4):
                    srcw = self.win[nm]
                    ncols = srcw.shape[1]
                    for r0 in range(ntile):
                        if rid >= 1:
                            sy.wait_ge(cvw, rid)
                        sy.dma_start(out=wstage[:, 0:ncols],
                                     in_=srcw[r0 * 128:(r0 + 1) * 128, :]).then_inc(ldw, 16)
                        rid += 1
                for g in range(n_tg):
                    b = g % NB
                    if g >= NB:
                        sy.wait_ge(rms, 2 * (g - NB) + 2)
                    for cchunk in range(8):
                        sy.dma_start(out=uf_hi[b][:, cchunk * 512:(cchunk + 1) * 512],
                                     in_=self.scr["u_hi"][g * 512:(g + 1) * 512,
                                                          cchunk * 128:(cchunk + 1) * 128],
                                     transpose=True).then_inc(ldu[b], 16)
                        sy.dma_start(out=uf_lo[b][:, cchunk * 512:(cchunk + 1) * 512],
                                     in_=self.scr["u_lo"][g * 512:(g + 1) * 512,
                                                          cchunk * 128:(cchunk + 1) * 128],
                                     transpose=True).then_inc(ldu[b], 16)

            @blk.gpsimd
            def _(gp):
                wrcnt = 0
                rid = 0
                for wi, (nm, thi, tlo, nrows, ntile) in enumerate(W4):
                    ncols = self.win[nm].shape[1]
                    for r0 in range(ntile):
                        gp.wait_ge(cvw, rid + 1)
                        if wi >= 1 and r0 == 0:
                            gp.wait_ge(trl, wi)
                        tls = [r0]
                        if nm in ("rope_k_w", "rope_q_w"):
                            for h in range(16):
                                for half in range(2):
                                    srow = 32 * h + 16 * half
                                    drow = 256 * half + 16 * h
                                    tl, off = divmod(srow, 128)
                                    if tl not in tls:
                                        continue
                                    tj = 0
                                    gp.dma_start(out=self.scr["wsthi"][drow:drow + 16, 0:ncols],
                                                 in_=wsplit_hi[off:off + 16, tj * ncols:tj * ncols + ncols]
                                                 ).then_inc(sws, 16)
                                    gp.dma_start(out=self.scr["wstlo"][drow:drow + 16, 0:ncols],
                                                 in_=wsplit_lo[off:off + 16, tj * ncols:tj * ncols + ncols]
                                                 ).then_inc(sws, 16)
                                    wrcnt += 32
                        else:
                            tl = r0
                            gp.dma_start(out=self.scr["wsthi"][tl * 128:(tl + 1) * 128, 0:ncols],
                                         in_=wsplit_hi[:, 0:ncols]).then_inc(sws, 16)
                            gp.dma_start(out=self.scr["wstlo"][tl * 128:(tl + 1) * 128, 0:ncols],
                                         in_=wsplit_lo[:, 0:ncols]).then_inc(sws, 16)
                            wrcnt += 32
                        gp.wait_ge(sws, wrcnt)
                        gp.nop().then_inc(stg4, 1)
                        rid += 1
                for g in range(n_tg):
                    pr = g % 2
                    gp.wait_ge(rvs, 2 * g + 1)
                    for m in range(4):
                        gp.dma_start(out=self.scr["krot_hi"][m * 128:(m + 1) * 128, g * 512:(g + 1) * 512],
                                     in_=rot_hi[:, m * 512:(m + 1) * 512]).then_inc(ros[pr], 16)
                    for m in range(4):
                        gp.dma_start(out=self.scr["krot_lo"][m * 128:(m + 1) * 128, g * 512:(g + 1) * 512],
                                     in_=rot_lo[:, m * 512:(m + 1) * 512]).then_inc(ros[pr], 16)
                    gp.wait_ge(rvs, 2 * g + 2)
                    for m in range(4):
                        gp.dma_start(out=self.scr["qrot_hi"][m * 128:(m + 1) * 128, g * 512:(g + 1) * 512],
                                     in_=rot_hi[:, m * 512:(m + 1) * 512]).then_inc(ros[pr], 16)
                    for m in range(4):
                        gp.dma_start(out=self.scr["qrot_lo"][m * 128:(m + 1) * 128, g * 512:(g + 1) * 512],
                                     in_=rot_lo[:, m * 512:(m + 1) * 512]).then_inc(ros[pr], 16)
                for pr in range(2):
                    nuse = len([g for g in range(n_tg) if g % 2 == pr])
                    gp.wait_ge(ros[pr], 256 * nuse)

            @blk.scalar
            def _(sc):
                ltot = 0
                cumr = []
                acc = 0
                for _, _, _, _, ntile in W4:
                    acc += ntile
                    cumr.append(acc)
                for wi, (nm, thi, tlo, nrows, ntile) in enumerate(W4):
                    sc.wait_ge(stg4, cumr[wi])
                    ncols = self.win[nm].shape[1]
                    nin, nout = ncols, nrows
                    for cchunk in range(nin // 128):
                        sc.dma_start(out=thi[:, cchunk * nout:(cchunk + 1) * nout],
                                     in_=self.scr["wsthi"][0:nout, cchunk * 128:(cchunk + 1) * 128],
                                     transpose=True).then_inc(ldt, 16)
                        sc.dma_start(out=tlo[:, cchunk * nout:(cchunk + 1) * nout],
                                     in_=self.scr["wstlo"][0:nout, cchunk * 128:(cchunk + 1) * 128],
                                     transpose=True).then_inc(ldt, 16)
                        ltot += 32
                    sc.wait_ge(ldt, ltot)
                    sc.nop().then_inc(trl, 1)

            @blk.vector
            def _(v):
                rid = 0
                ldcum = 0
                for wi, (nm, thi, tlo, nrows, ntile) in enumerate(W4):
                    ncols = self.win[nm].shape[1]
                    for r0 in range(ntile):
                        ldcum += 16
                        v.wait_ge(ldw, ldcum)
                        if rid >= 1:
                            v.wait_ge(stg4, rid)
                        nel = ncols
                        v.tensor_copy(wsplit_hi[:, 0:nel], wstage[:, 0:nel])
                        v.drain()
                        v.tensor_sub(wsplit_lo[:, 0:nel], wstage[:, 0:nel], wsplit_hi[:, 0:nel])
                        v.drain()
                        v.nop().then_inc(cvw, 1)
                        rid += 1
                iE = 0
                for g in range(n_tg):
                    for tens, this_hi, this_lo in ((0, self.kv_lat_hi, self.kv_lat_lo),
                                                   (1, self.q_lat_hi, self.q_lat_lo)):
                        for m in range(2):
                            v.wait_ge(mms, 4 * g + 2 * tens + m + 1)
                            dst_h = this_hi[:, m * self.T + g * 512: m * self.T + (g + 1) * 512]
                            dst_l = this_lo[:, m * self.T + g * 512: m * self.T + (g + 1) * 512]
                            v.tensor_copy(dst_h, lat_ps[m % 2][:])
                            v.drain()
                            v.tensor_tensor(dst_l, lat_ps[m % 2][:], dst_h, ALU.subtract)
                            v.drain()
                            v.nop().then_inc(evs, 1)
                    for rr in range(2):
                        v.wait_ge(rms, 2 * g + rr + 1)
                        if rr == 0 and g >= 1:
                            v.wait_ge(ros[(g - 1) % 2], 256 * ((g - 1) // 2 + 1))
                        if rr == 1:
                            v.wait_ge(ros[g % 2], 256 * (g // 2) + 128)
                        for m in range(4):
                            v.tensor_copy(rot32[m][:], rot_ps[m][:])
                        v.drain()
                        cosg = self.cosE[:, g * 512:(g + 1) * 512]
                        sing = self.sinE[:, g * 512:(g + 1) * 512]
                        for a in range(2):
                            x1, x2 = rot32[a], rot32[a + 2]
                            v.tensor_mul(rt32[0][:], x2[:], sing)
                            v.tensor_mul(rt32[1][:], x1[:], sing)
                            v.drain()
                            v.tensor_mul(x1[:], x1[:], cosg)
                            v.tensor_mul(x2[:], x2[:], cosg)
                            v.drain()
                            v.tensor_sub(x1[:], x1[:], rt32[0][:])
                            v.tensor_add(x2[:], x2[:], rt32[1][:])
                            v.drain()
                        for m in range(4):
                            v.tensor_copy(rot_hi[:, m * 512:(m + 1) * 512], rot32[m][:])
                        v.drain()
                        for m in range(4):
                            v.tensor_tensor(rot_lo[:, m * 512:(m + 1) * 512], rot32[m][:],
                                            rot_hi[:, m * 512:(m + 1) * 512], ALU.subtract)
                        v.drain()
                        v.nop().then_inc(rvs, 1)

            @blk.tensor
            def _(t):
                t.wait_ge(trl, 4)
                for g in range(n_tg):
                    b = g % NB
                    t.wait_ge(ldu[b], 16 * 16 * (g // NB + 1))
                    for tens, whi, wlo in ((0, kvdT_hi, kvdT_lo), (1, qdT_hi, qdT_lo)):
                        for m in range(2):
                            gidx = 4 * g + 2 * tens + m
                            if gidx >= 2:
                                t.wait_ge(evs, gidx - 1)
                            ps = lat_ps[m % 2]
                            for cchunk in range(8):
                                lh = whi[:, cchunk * 256 + m * 128: cchunk * 256 + m * 128 + 128]
                                ll = wlo[:, cchunk * 256 + m * 128: cchunk * 256 + m * 128 + 128]
                                rh = uf_hi[b][:, cchunk * 512:(cchunk + 1) * 512]
                                rl = uf_lo[b][:, cchunk * 512:(cchunk + 1) * 512]
                                self.fp32mm(t, ps[:], lh, ll, rh, rl,
                                            start=(cchunk == 0), stop=(cchunk == 7),
                                            inc=mms if cchunk == 7 else None)
                    if g >= 1:
                        t.wait_ge(rvs, 2 * g)
                    for m in range(4):
                        ps = rot_ps[m]
                        for cchunk in range(8):
                            lh = rkT_hi[:, cchunk * 512 + m * 128: cchunk * 512 + m * 128 + 128]
                            ll = rkT_lo[:, cchunk * 512 + m * 128: cchunk * 512 + m * 128 + 128]
                            rh = uf_hi[b][:, cchunk * 512:(cchunk + 1) * 512]
                            rl = uf_lo[b][:, cchunk * 512:(cchunk + 1) * 512]
                            self.fp32mm(t, ps[:], lh, ll, rh, rl,
                                        start=(cchunk == 0), stop=(cchunk == 7),
                                        inc=rms if (cchunk == 7 and m == 3) else None)
                    t.wait_ge(evs, 4 * g + 4)
                    t.wait_ge(rvs, 2 * g + 1)
                    for m in range(4):
                        ps = rot_ps[m]
                        for lc in range(2):
                            lh = rqT_hi[:, lc * 512 + m * 128: lc * 512 + m * 128 + 128]
                            ll = rqT_lo[:, lc * 512 + m * 128: lc * 512 + m * 128 + 128]
                            rh = self.q_lat_hi[:, lc * self.T + g * 512: lc * self.T + (g + 1) * 512]
                            rl = self.q_lat_lo[:, lc * self.T + g * 512: lc * self.T + (g + 1) * 512]
                            self.fp32mm(t, ps[:], lh, ll, rh, rl,
                                        start=(lc == 0), stop=(lc == 1),
                                        inc=rms if (lc == 1 and m == 3) else None)
        self.end_phase()
        st.close()

    def p4_inner_attn(self):
        nc, T, S, C, NH = self.nc, self.T, self.S, self.C, self.NH
        n_qg = S // 512
        st = ExitStack()
        self.masks = self.sb("p4_masks", [128, 6 * 512], BF16, st)
        mcv = self.sem("p4_mcv"); mga = self.sem("p4_mga")
        with nc.Block() as mblk:
            self.emit_masks(mblk.vector, mblk.gpsimd, self.masks, mcv, mga)
        nc.all_engine_barrier()
        wstage = self.sb("p4_wstage", [128, 256], F32, st)
        kuT_hi = self.sb("p4_kuT_hi", [128, 2 * C], BF16, st)
        kuT_lo = self.sb("p4_kuT_lo", [128, 2 * C], BF16, st)
        quT_hi = self.sb("p4_quT_hi", [128, 2 * C], BF16, st)
        quT_lo = self.sb("p4_quT_lo", [128, 2 * C], BF16, st)
        vuT_hi = self.sb("p4_vuT_hi", [128, 2 * C], BF16, st)
        vuT_lo = self.sb("p4_vuT_lo", [128, 2 * C], BF16, st)
        ldw = self.sem("p4_ldw"); spl = self.sem("p4_spl")
        stg4 = self.sem("p4_stg"); trl = self.sem("p4_trl")
        NB = 2
        kst = [self.sb(f"p4_kst{i}", [128, T], BF16, st) for i in range(NB)]
        vI = [self.sb(f"p4_v{i}", [128, T], BF16, st) for i in range(NB)]
        qr1 = [self.sb(f"p4_qr1{i}", [128, S], BF16, st) for i in range(NB)]
        qr2 = [self.sb(f"p4_qr2{i}", [64, S], BF16, st) for i in range(NB)]
        ldh = [self.sem(f"p4_ld{i}") for i in range(NB)]
        NP = 3
        p32 = [self.sb(f"p4_p32{i}", [128, 512], F32, st) for i in range(NP)]
        ph = [self.sb(f"p4_ph{i}", [128, 512], BF16, st) for i in range(NP)]
        pl = [self.sb(f"p4_pl{i}", [128, 512], BF16, st) for i in range(NP)]
        ones128 = self.sb("p4_ones128", [128, 1], BF16, st)
        ones1 = self.sb("p4_ones1", [1, 64], F32, st)
        ost = [self.sb(f"p4_ost{i}", [128, 512], BF16, st) for i in range(2)]
        o32 = [self.sb(f"p4_o32{i}", [64, 512], F32, st) for i in range(2)]
        rlt = [self.sb(f"p4_rl{i}", [1, 512], F32, st) for i in range(2)]
        s_ps = [self.psum(f"p4_sps{i}", [128, 512], F32, st) for i in range(2)]
        oA_ps = [self.psum(f"p4_oA{i}", [128, 512], F32, st) for i in range(2)]
        oB_ps = [self.psum(f"p4_oB{i}", [64, 512], F32, st) for i in range(2)]
        l_ps = self.psum("p4_l", [1, 512], F32, st)
        rl_ps = self.psum("p4_rlb", [64, 512], F32, st)
        RA = oA_ps[0][0:32, :]
        RB = oB_ps[0][0:32, :]
        VA = oA_ps[0][:, 0:64]
        VB = oA_ps[1][:, 0:64]
        sS = self.sem("p4_sS"); sM = self.sem("p4_sM"); sE = self.sem("p4_sE")
        sP = self.sem("p4_sP"); sA = self.sem("p4_sA"); sG = self.sem("p4_sG")
        sR = self.sem("p4_sR"); sL = self.sem("p4_sL")
        sPr = self.sem("p4_sPr")
        sPv = self.sem("p4_sPv")
        sQO = self.sem("p4_sQO")
        sRM = self.sem("p4_sRM")
        qlo_hi, qlo_lo = self.qlo_hi, self.qlo_lo
        rmask = self.sb("p4_rmask", [128, 8 * 512], BF16, st)
        biascur = self.sb("p4_biascur", [128, 1], F32, st)
        soh = [self.sem(f"p4_so{i}") for i in range(2)]
        def ktmax(g):
            return S // 128 + 4 * (g + 1)
        def klass(g, kt):
            j0 = kt * 128 - g * 512
            j1 = j0 - S
            if j0 <= -128:
                return ("vis", None)
            if 0 <= j0 < 512:
                return ("diag0", j0 // 128)
            if 0 <= j1 < 512:
                return ("diag1", j1 // 128)
            return ("hid0", None)
        ktph = sum(ktmax(g) for g in range(n_qg))
        n_tk5 = T // 512
        nG = NH * n_qg
        mask_ord = {}
        sMc = 0
        kk = 0
        for h in range(NH):
            for g in range(n_qg):
                for kt in range(ktmax(g)):
                    kl, jj = klass(g, kt)
                    if kl in ("diag0", "diag1"):
                        sMc += 1
                        mask_ord[kk] = sMc
                    kk += 1
        cumA = []
        acc = 0
        for h in range(NH):
            for g in range(n_qg):
                acc += ktmax(g)
                cumA.append(acc)
        with nc.Block() as blk:
            @blk.sync
            def _(sy):
                role_v = sy.value_load(self.role_sb[0:1, 0:1])
                q0v = sy.snap(role_v * S)
                sy.dma_start(out=self.scr["qrotO_hi"],
                             in_=self.scr["qrot_hi"][:, bass.ds(q0v, S)]).then_inc(sQO, 16)
                sy.dma_start(out=self.scr["qrotO_lo"],
                             in_=self.scr["qrot_lo"][:, bass.ds(q0v, S)]).then_inc(sQO, 16)
                for wi, nm in enumerate(("k_u", "q_u", "v_u")):
                    for tl in range(8):
                        rid = 8 * wi + tl
                        if rid >= 1:
                            sy.wait_ge(spl, rid)
                        sy.dma_start(out=wstage[:],
                                     in_=self.win[nm][tl * 128:(tl + 1) * 128, :]).then_inc(ldw, 16)
                for h in range(NH):
                    b = h % NB
                    if h >= NB:
                        sy.wait_ge(sA, ktph * (h - NB + 1))
                    sy.dma_start(out=kst[b][0:16, :],
                                 in_=self.scr["krot_hi"][16 * h:16 * h + 16, :]).then_inc(ldh[b], 16)
                    sy.dma_start(out=kst[b][16:32, :],
                                 in_=self.scr["krot_hi"][256 + 16 * h:256 + 16 * h + 16, :]).then_inc(ldh[b], 16)
                    sy.dma_start(out=kst[b][64:80, :],
                                 in_=self.scr["krot_lo"][16 * h:16 * h + 16, :]).then_inc(ldh[b], 16)
                    sy.dma_start(out=kst[b][80:96, :],
                                 in_=self.scr["krot_lo"][256 + 16 * h:256 + 16 * h + 16, :]).then_inc(ldh[b], 16)
                    if h == 0:
                        sy.wait_ge(sQO, 32)
                    sy.dma_start(out=qr1[b][0:16, :],
                                 in_=self.scr["qrotO_hi"][16 * h:16 * h + 16, :]).then_inc(ldh[b], 16)
                    sy.dma_start(out=qr1[b][16:32, :],
                                 in_=self.scr["qrotO_hi"][256 + 16 * h:256 + 16 * h + 16, :]).then_inc(ldh[b], 16)
                    sy.dma_start(out=qr1[b][64:80, :],
                                 in_=self.scr["qrotO_hi"][16 * h:16 * h + 16, :]).then_inc(ldh[b], 16)
                    sy.dma_start(out=qr1[b][80:96, :],
                                 in_=self.scr["qrotO_hi"][256 + 16 * h:256 + 16 * h + 16, :]).then_inc(ldh[b], 16)
                    sy.dma_start(out=qr2[b][0:16, :],
                                 in_=self.scr["qrotO_lo"][16 * h:16 * h + 16, :]).then_inc(ldh[b], 16)
                    sy.dma_start(out=qr2[b][16:32, :],
                                 in_=self.scr["qrotO_lo"][256 + 16 * h:256 + 16 * h + 16, :]).then_inc(ldh[b], 16)

            @blk.vector
            def _(v):
                v.memset(ones128[:], 1.0)
                v.memset(ones1[:], 1.0)
                v.drain()
                for wi, (thi, tlo) in enumerate(((kuT_hi, kuT_lo), (quT_hi, quT_lo),
                                                 (vuT_hi, vuT_lo))):
                    for tl in range(8):
                        rid = 8 * wi + tl
                        v.wait_ge(ldw, 16 * (rid + 1))
                        v.tensor_copy(thi[:, tl * 256:(tl + 1) * 256], wstage[:])
                        v.drain()
                        v.tensor_sub(tlo[:, tl * 256:(tl + 1) * 256], wstage[:],
                                     thi[:, tl * 256:(tl + 1) * 256])
                        v.drain()
                        v.nop().then_inc(spl, 1)
                for di in range(4):
                    v.tensor_scalar_mul(rmask[:, di * 512:(di + 1) * 512],
                                        self.masks[:, (1 + di) * 512:(2 + di) * 512],
                                        self.rolemB[:, 0:1])
                    v.tensor_scalar_mul(rmask[:, (4 + di) * 512:(5 + di) * 512],
                                        self.masks[:, (1 + di) * 512:(2 + di) * 512],
                                        self.rolemB[:, 1:2])
                v.tensor_scalar(biascur[:], self.rolemB[:, 1:2], -SHIFT, None, ALU.mult)
                v.drain()
                v.tensor_scalar(biascur[:], self.rolemB[:, 0:1], NEGBIG, biascur[:, 0:1],
                                ALU.mult, ALU.add)
                v.drain()
                v.nop().then_inc(sRM, 1)
                iPr = 0
                k = 0
                iG = 0
                for h in range(NH):
                    b = h % NB
                    pidx = 0
                    for gk in range(n_tk5):
                        reg = (RA, RB)[pidx % 2]
                        iPr += 1
                        v.wait_ge(sPr, iPr)
                        dsth = kst[b][32:64, gk * 512:(gk + 1) * 512]
                        dstl = kst[b][96:128, gk * 512:(gk + 1) * 512]
                        v.tensor_copy(dsth, reg)
                        v.drain()
                        v.tensor_tensor(dstl, reg, dsth, ALU.subtract).then_inc(sPv, 1)
                        pidx += 1
                    for gq in range(n_qg):
                        reg = (RA, RB)[pidx % 2]
                        iPr += 1
                        v.wait_ge(sPr, iPr)
                        dsth = qr1[b][32:64, gq * 512:(gq + 1) * 512]
                        dstl = qr2[b][32:64, gq * 512:(gq + 1) * 512]
                        v.tensor_copy(dsth, reg)
                        v.drain()
                        v.tensor_tensor(dstl, reg, dsth, ALU.subtract)
                        v.tensor_copy(qr1[b][96:128, gq * 512:(gq + 1) * 512],
                                      dsth).then_inc(sPv, 1)
                        pidx += 1
                    for a in range(T // 128):
                        reg = (VA, VB)[a % 2]
                        iPr += 1
                        v.wait_ge(sPr, iPr)
                        dsth = vI[b][:, a * 128:a * 128 + 64]
                        dstl = vI[b][:, a * 128 + 64:a * 128 + 128]
                        v.tensor_copy(dsth, reg)
                        v.drain()
                        v.tensor_tensor(dstl, reg, dsth, ALU.subtract).then_inc(sPv, 1)
                    for g in range(n_qg):
                        nkt = ktmax(g)
                        pg = iG % 2
                        for kt in range(nkt):
                            pi = k % NP
                            kl, jj = klass(g, kt)
                            if kl in ("diag0", "diag1"):
                                v.wait_ge(sS, k + 1)
                                mcol = jj if kl == "diag0" else 4 + jj
                                v.tensor_tensor(s_ps[k % 2][:], s_ps[k % 2][:],
                                                rmask[:, mcol * 512:(mcol + 1) * 512],
                                                ALU.add).then_inc(sM, 1)
                            v.wait_ge(sE, 2 * k + 2)
                            if k >= NP:
                                v.wait_ge(sA, k - NP + 1)
                            v.tensor_sub(pl[pi][:], p32[pi][:], ph[pi][:]).then_inc(sP, 1)
                            k += 1
                        v.wait_ge(sA, cumA[iG])
                        gb = iG % 2
                        v.wait_ge(soh[gb], 32 * (iG // 2))
                        v.tensor_copy(o32[pg][:], oA_ps[pg][0:64, :])
                        v.tensor_copy(rlt[pg][:], l_ps[0:1, :])
                        v.drain()
                        v.tensor_add(o32[pg][:], o32[pg][:], oA_ps[pg][64:128, :])
                        v.reciprocal(rlt[pg][:], rlt[pg][:]).then_inc(sL, 1)
                        v.drain()
                        v.tensor_add(o32[pg][:], o32[pg][:], oB_ps[pg][0:64, :])
                        v.drain()
                        v.wait_ge(sR, iG + 1)
                        v.tensor_mul(o32[pg][:], o32[pg][:], rl_ps[0:64, :])
                        v.drain()
                        v.tensor_copy(ost[gb][0:64, :], o32[pg][:])
                        v.drain()
                        v.tensor_tensor(ost[gb][64:128, :], o32[pg][:], ost[gb][0:64, :],
                                        ALU.subtract).then_inc(sG, 2)
                        iG += 1

            @blk.gpsimd
            def _(gp):
                for wi in range(3):
                    if wi >= 1:
                        gp.wait_ge(trl, wi)
                    thi = (kuT_hi, quT_hi, vuT_hi)[wi]
                    tlo = (kuT_lo, quT_lo, vuT_lo)[wi]
                    for tl in range(8):
                        gp.wait_ge(spl, 8 * wi + tl + 1)
                        gp.dma_start(out=self.scr["wsthi"][tl * 128:(tl + 1) * 128, 0:256],
                                     in_=thi[:, tl * 256:(tl + 1) * 256]).then_inc(stg4, 16)
                        gp.dma_start(out=self.scr["wstlo"][tl * 128:(tl + 1) * 128, 0:256],
                                     in_=tlo[:, tl * 256:(tl + 1) * 256]).then_inc(stg4, 16)
                    gp.wait_ge(stg4, 256 * (wi + 1))
                iG = 0
                for h in range(NH):
                    for g in range(n_qg):
                        gb = iG % 2
                        gp.wait_ge(sG, 2 * iG + 2)
                        gp.dma_start(out=self.scr["oB_hi"][64 * h:64 * h + 64,
                                                           g * 512:(g + 1) * 512],
                                     in_=ost[gb][0:64, :]).then_inc(soh[gb], 16)
                        gp.dma_start(out=self.scr["oB_lo"][64 * h:64 * h + 64,
                                                           g * 512:(g + 1) * 512],
                                     in_=ost[gb][64:128, :]).then_inc(soh[gb], 16)
                        iG += 1
                for p in range(2):
                    nuse = len([i for i in range(NH * n_qg) if i % 2 == p])
                    gp.wait_ge(soh[p], 32 * nuse)

            @blk.scalar
            def _(sc):
                sc.wait_ge(sRM, 1)
                for wi, (thi, tlo) in enumerate(((kuT_hi, kuT_lo), (quT_hi, quT_lo),
                                                 (vuT_hi, vuT_lo))):
                    sc.wait_ge(stg4, 256 * (wi + 1))
                    for cchunk in range(2):
                        sc.dma_start(out=thi[:, cchunk * C:(cchunk + 1) * C],
                                     in_=self.scr["wsthi"][0:C, cchunk * 128:(cchunk + 1) * 128],
                                     transpose=True).then_inc(ldw, 16)
                        sc.dma_start(out=tlo[:, cchunk * C:(cchunk + 1) * C],
                                     in_=self.scr["wstlo"][0:C, cchunk * 128:(cchunk + 1) * 128],
                                     transpose=True).then_inc(ldw, 16)
                    sc.wait_ge(ldw, 16 * 24 + 16 * 4 * (wi + 1))
                    sc.nop().then_inc(trl, 1)
                k = 0
                for h in range(NH):
                    for g in range(n_qg):
                        nkt = ktmax(g)
                        for kt in range(nkt):
                            kl, jj = klass(g, kt)
                            sc.wait_ge(sS, k + 1)
                            if kl in ("diag0", "diag1"):
                                sc.wait_ge(sM, mask_ord[k])
                            pi = k % NP
                            if k >= NP:
                                sc.wait_ge(sP, k - NP + 1)
                                sc.wait_ge(sA, k - NP + 1)
                            if kl in ("hid0", "diag1"):
                                bias_ap = biascur[:, 0:1]
                            else:
                                bias_ap = self.consts[:, 0:1]
                            sc.activation(p32[pi][:], s_ps[k % 2][:], AF.Exp,
                                          bias=bias_ap,
                                          scale=self.consts[:, 5:6]).then_inc(sE, 1)
                            sc.activation(ph[pi][:], s_ps[k % 2][:], AF.Exp,
                                          bias=bias_ap,
                                          scale=self.consts[:, 5:6]).then_inc(sE, 1)
                            k += 1

            @blk.tensor
            def _(t):
                t.wait_ge(trl, 3)
                t.wait_ge(sRM, 1)
                iG = 0
                k = [0]
                iPr = [0]
                for h in range(NH):
                    b = h % NB
                    t.wait_ge(ldh[b], 160 * (h // NB + 1))
                    ev = n_qg * h - 2 if (n_qg * h) % 2 == 0 else n_qg * h - 1
                    if ev >= 0:
                        t.wait_ge(sG, 2 * (ev + 1))
                    pidx = 0
                    for gk in range(n_tk5):
                        reg = (RA, RB)[pidx % 2]
                        if iPr[0] >= 2:
                            t.wait_ge(sPv, iPr[0] - 1)
                        for lc in range(2):
                            lh = kuT_hi[:, lc * C + 64 * h + 32: lc * C + 64 * h + 64]
                            ll = kuT_lo[:, lc * C + 64 * h + 32: lc * C + 64 * h + 64]
                            rh = self.kv_lat_hi[:, lc * T + gk * 512:lc * T + (gk + 1) * 512]
                            rl = self.kv_lat_lo[:, lc * T + gk * 512:lc * T + (gk + 1) * 512]
                            self.fp32mm(t, reg, lh, ll, rh, rl,
                                        start=(lc == 0), stop=(lc == 1),
                                        inc=sPr if lc == 1 else None)
                        iPr[0] += 1; pidx += 1
                    for gq in range(n_qg):
                        reg = (RA, RB)[pidx % 2]
                        if iPr[0] >= 2:
                            t.wait_ge(sPv, iPr[0] - 1)
                        for lc in range(2):
                            lh = quT_hi[:, lc * C + 64 * h + 32: lc * C + 64 * h + 64]
                            ll = quT_lo[:, lc * C + 64 * h + 32: lc * C + 64 * h + 64]
                            rh = qlo_hi[:, lc * S + gq * 512:lc * S + (gq + 1) * 512]
                            rl = qlo_lo[:, lc * S + gq * 512:lc * S + (gq + 1) * 512]
                            self.fp32mm(t, reg, lh, ll, rh, rl,
                                        start=(lc == 0), stop=(lc == 1),
                                        inc=sPr if lc == 1 else None)
                        iPr[0] += 1; pidx += 1
                    for a in range(T // 128):
                        reg = (VA, VB)[a % 2]
                        if a < 2:
                            t.wait_ge(sPv, iPr[0])
                            if a == 1 and h >= 1:
                                t.wait_ge(sG, 2 * n_qg * h)
                        elif iPr[0] >= 2:
                            t.wait_ge(sPv, iPr[0] - 1)
                        for lc in range(2):
                            lh_hi = self.kv_lat_hi[:, lc * T + a * 128:lc * T + (a + 1) * 128]
                            lh_lo = self.kv_lat_lo[:, lc * T + a * 128:lc * T + (a + 1) * 128]
                            rh = vuT_hi[:, lc * C + 64 * h: lc * C + 64 * h + 64]
                            rl = vuT_lo[:, lc * C + 64 * h: lc * C + 64 * h + 64]
                            self.fp32mm(t, reg, lh_hi, lh_lo, rh, rl,
                                        start=(lc == 0), stop=(lc == 1),
                                        inc=sPr if lc == 1 else None)
                        iPr[0] += 1
                    t.wait_ge(sPv, iPr[0])
                    for g in range(n_qg):
                        nkt = ktmax(g)
                        pg = iG % 2
                        k0 = k[0]

                        def emS(j):
                            kk = k0 + j
                            if kk >= 2:
                                t.wait_ge(sE, 2 * (kk - 1))
                            t.matmul(s_ps[kk % 2][:], kst[b][:, j * 128:(j + 1) * 128],
                                     qr1[b][:, g * 512:(g + 1) * 512],
                                     start=True, stop=False)
                            t.matmul(s_ps[kk % 2][:], kst[b][0:64, j * 128:(j + 1) * 128],
                                     qr2[b][:, g * 512:(g + 1) * 512],
                                     start=False, stop=True).then_inc(sS, 1)

                        def emA(j):
                            kk = k0 + j
                            t.wait_ge(sP, kk + 1)
                            if j == 0:
                                if iG >= 2:
                                    t.wait_ge(sG, 2 * (iG - 1))
                                if iG >= 1:
                                    t.wait_ge(sL, iG)
                            vsl = vI[b][:, j * 128:(j + 1) * 128]
                            pi = kk % NP
                            t.matmul(oA_ps[pg][:], vsl, ph[pi][:],
                                     start=(j == 0), stop=(j == nkt - 1))
                            t.matmul(oB_ps[pg][:], vsl[:, 0:64], pl[pi][:],
                                     start=(j == 0), stop=(j == nkt - 1))
                            t.matmul(l_ps[0:1, :], ones128[:], ph[pi][:],
                                     start=(j == 0), stop=False)
                            t.matmul(l_ps[0:1, :], ones128[:], pl[pi][:],
                                     start=False, stop=(j == nkt - 1)).then_inc(sA, 1)
                        emS(0); emS(1)
                        if g >= 1:
                            t.wait_ge(sL, iG)
                            t.matmul(rl_ps[:], ones1[:], rlt[(iG - 1) % 2][:], start=True,
                                     stop=True).then_inc(sR, 1)
                        for j in range(nkt):
                            emA(j)
                            if j + 2 < nkt:
                                emS(j + 2)
                        k[0] += nkt
                        if g == n_qg - 1:
                            t.wait_ge(sL, iG + 1)
                            t.matmul(rl_ps[:], ones1[:], rlt[iG % 2][:], start=True,
                                     stop=True).then_inc(sR, 1)
                        iG += 1
        self.end_phase()
        st.close()

    def p5_z1_router(self):
        nc, T, S, C, NH = self.nc, self.T, self.S, self.C, self.NH
        n_tt = S // 128
        st0 = ExitStack()
        cps = self.sem("p5_cps")
        if not self.dense:
            self.zg = self.sem("p5_zg", persist=True)
            zb16 = self.sb("p5_zb16", [128, C], BF16, st0)
            zb32 = self.sb("p5_zb32", [128, C], F32, st0)
            NSLOT = self.ER * self.CE + 128
        with nc.Block() as blk0:
            @blk0.sync
            def _(sy):
                role_v = sy.value_load(self.role_sb[0:1, 0:1])
                q0v = sy.snap(role_v * S)
                sy.dma_start(out=self.scr["y2O"],
                             in_=self.scr["y2_sc"][bass.ds(q0v, S), :]).then_inc(cps, 16)
                sy.dma_start(out=self.scr["xmidO"],
                             in_=self.scr["xmid_sc"][bass.ds(q0v, S), :]).then_inc(cps, 16)
                sy.wait_ge(cps, 32)

            if not self.dense:
                zmm = self.sem("p5_zmm")

                @blk0.vector
                def _(v):
                    v.memset(zb16[:], 0.0)
                    v.memset(zb32[:], 0.0).then_inc(zmm, 1)

                @blk0.gpsimd
                def _(gp):
                    gp.wait_ge(zmm, 1)
                    nz = NSLOT // 128
                    for a in range(nz):
                        gp.dma_start(out=self.scr["gsc_hi"][a * 128:(a + 1) * 128, :],
                                     in_=zb16[:]).then_inc(self.zg, 16)
                        gp.dma_start(out=self.scr["gsc_lo"][a * 128:(a + 1) * 128, :],
                                     in_=zb16[:]).then_inc(self.zg, 16)
                    gp.dma_start(out=self.scr["wgtS_sc"].rearrange("(p a) c -> p (a c)", p=128),
                                 in_=zb32[:, 0:nz]).then_inc(self.zg, 16)
                    gp.dma_start(out=self.scr["resbuf"][NSLOT - 128:NSLOT, :],
                                 in_=zb32[:]).then_inc(self.zg, 16)
                    gp.wait_ge(self.zg, 16 * (2 * nz + 2))
        self.end_phase()
        st0.close()

        st = ExitStack()
        wo_hi = self.sb("p5_wo_hi", [128, 8 * C], BF16, st)
        wo_lo = self.sb("p5_wo_lo", [128, 8 * C], BF16, st)
        wstage = self.sb("p5_wstage", [128, C], F32, st)
        ldw = self.sem("p5_ldw"); spl = self.sem("p5_spl")
        stg = self.sem("p5_stg"); trl = self.sem("p5_trl")
        NB = 2
        oT = [self.sb(f"p5_oT{i}", [128, 16 * 128], BF16, st) for i in range(NB)]
        y2t = [self.sb(f"p5_y2t{i}", [128, C], F32, st) for i in range(NB)]
        xmt = [self.sb(f"p5_xmt{i}", [128, C], F32, st) for i in range(NB)]
        z1t = [self.sb(f"p5_z1t{i}", [128, C], F32, st) for i in range(NB)]
        s1t = [self.sb(f"p5_s1t{i}", [128, C], F32, st) for i in range(NB)]
        op_ps = [self.psum(f"p5_ps{i}", [128, 512], F32, st) for i in range(2)]
        ldo = [self.sem(f"p5_ldo{i}") for i in range(NB)]
        ldx = [self.sem(f"p5_ldx{i}") for i in range(NB)]
        mms = self.sem("p5_mm"); evs = self.sem("p5_ev")
        sos = [self.sem(f"p5_so{i}") for i in range(NB)]
        with nc.Block() as blk:
            @blk.sync
            def _(sy):
                for cchunk in range(8):
                    if cchunk >= 1:
                        sy.wait_ge(spl, cchunk)
                    sy.dma_start(out=wstage[:],
                                 in_=self.win["o_w"][cchunk * 128:(cchunk + 1) * 128, :]
                                 ).then_inc(ldw, 16)
                for i in range(n_tt):
                    b = i % NB
                    if i >= NB:
                        sy.wait_ge(mms, 2 * (i - NB + 1))
                    for cchunk in range(8):
                        sy.dma_start(out=oT[b][:, cchunk * 128:(cchunk + 1) * 128],
                                     in_=self.scr["oB_hi"][cchunk * 128:(cchunk + 1) * 128,
                                                           i * 128:(i + 1) * 128]).then_inc(ldo[b], 16)
                        sy.dma_start(out=oT[b][:, 1024 + cchunk * 128:1024 + (cchunk + 1) * 128],
                                     in_=self.scr["oB_lo"][cchunk * 128:(cchunk + 1) * 128,
                                                           i * 128:(i + 1) * 128]).then_inc(ldo[b], 16)
                    sy.wait_ge(evs, max(0, (i - NB + 1)))
                    sy.dma_start(out=y2t[b][:],
                                 in_=self.scr["y2O"][i * 128:(i + 1) * 128, :]).then_inc(ldx[b], 16)
                    sy.dma_start(out=xmt[b][:],
                                 in_=self.scr["xmidO"][i * 128:(i + 1) * 128, :]).then_inc(ldx[b], 16)

            @blk.gpsimd
            def _(gp):
                for cchunk in range(8):
                    gp.wait_ge(spl, cchunk + 1)
                    gp.dma_start(out=self.scr["wsthi"][cchunk * 128:(cchunk + 1) * 128, 0:C],
                                 in_=wo_hi[:, cchunk * C:(cchunk + 1) * C]).then_inc(stg, 16)
                    gp.dma_start(out=self.scr["wstlo"][cchunk * 128:(cchunk + 1) * 128, 0:C],
                                 in_=wo_lo[:, cchunk * C:(cchunk + 1) * C]).then_inc(stg, 16)

            @blk.scalar
            def _(sc):
                sc.wait_ge(stg, 256)
                for cchunk in range(8):
                    sc.dma_start(out=wo_hi[:, cchunk * C:(cchunk + 1) * C],
                                 in_=self.scr["wsthi"][0:C, cchunk * 128:(cchunk + 1) * 128],
                                 transpose=True).then_inc(ldw, 16)
                    sc.dma_start(out=wo_lo[:, cchunk * C:(cchunk + 1) * C],
                                 in_=self.scr["wstlo"][0:C, cchunk * 128:(cchunk + 1) * 128],
                                 transpose=True).then_inc(ldw, 16)
                sc.wait_ge(ldw, 16 * 8 + 16 * 16)
                sc.nop().then_inc(trl, 1)
                for i in range(n_tt):
                    b = i % NB
                    sc.wait_ge(evs, i + 1)
                    sc.dma_start(out=self.scr["z1O"][i * 128:(i + 1) * 128, :],
                                 in_=z1t[b][:]).then_inc(sos[b], 16)
                    sc.dma_start(out=self.scr["s1_sc"][i * 128:(i + 1) * 128, :],
                                 in_=s1t[b][:]).then_inc(sos[b], 16)
                for b in range(NB):
                    nfull = len([i for i in range(n_tt) if i % NB == b])
                    sc.wait_ge(sos[b], 32 * nfull)

            @blk.vector
            def _(v):
                for cchunk in range(8):
                    v.wait_ge(ldw, 16 * (cchunk + 1))
                    v.tensor_copy(wo_hi[:, cchunk * C:(cchunk + 1) * C], wstage[:])
                    v.drain()
                    v.tensor_sub(wo_lo[:, cchunk * C:(cchunk + 1) * C], wstage[:],
                                 wo_hi[:, cchunk * C:(cchunk + 1) * C])
                    v.drain()
                    v.nop().then_inc(spl, 1)
                for i in range(n_tt):
                    b = i % NB
                    if i >= NB:
                        v.wait_ge(sos[b], 32 * (i // NB))
                    v.wait_ge(mms, 2 * i + 1)
                    v.wait_ge(ldx[b], 32 * (i // NB + 1))
                    v.tensor_add(z1t[b][:, 0:512], op_ps[0][:], y2t[b][:, 0:512])
                    v.wait_ge(mms, 2 * i + 2)
                    v.tensor_add(z1t[b][:, 512:1024], op_ps[1][:], y2t[b][:, 512:1024])
                    v.drain()
                    v.tensor_add(s1t[b][:], z1t[b][:], xmt[b][:])
                    v.drain()
                    v.nop().then_inc(evs, 1)

            @blk.tensor
            def _(t):
                t.wait_ge(trl, 1)
                for i in range(n_tt):
                    b = i % NB
                    t.wait_ge(ldo[b], 16 * 16 * (i // NB + 1))
                    if i >= 1:
                        t.wait_ge(evs, i)
                    for half in range(2):
                        for cchunk in range(8):
                            lh = oT[b][:, cchunk * 128:(cchunk + 1) * 128]
                            ll = oT[b][:, 1024 + cchunk * 128:1024 + (cchunk + 1) * 128]
                            rh = wo_hi[:, cchunk * C + half * 512:cchunk * C + half * 512 + 512]
                            rl = wo_lo[:, cchunk * C + half * 512:cchunk * C + half * 512 + 512]
                            self.fp32mm(t, op_ps[half][:], lh, ll, rh, rl,
                                        start=(cchunk == 0), stop=(cchunk == 7),
                                        inc=mms if cchunk == 7 else None)
        self.end_phase()
        st.close()
        self._ln_stream(src_dram=self.scr["z1O"], dst_hi=self.scr["w2t_hi"],
                        dst_lo=self.scr["w2t_lo"], n_rows=S, wB=self.lnw2B, tag="p5b")
        st = ExitStack()
        rwT_hi = self.sb("p5c_rwT_hi", [128, 8 * 16], BF16, st)
        rwT_lo = self.sb("p5c_rwT_lo", [128, 8 * 16], BF16, st)
        rstage = self.sb("p5c_rstage", [16, C], F32, st)
        rsp_hi = self.sb("p5c_rsp_hi", [16, C], BF16, st)
        rsp_lo = self.sb("p5c_rsp_lo", [16, C], BF16, st)
        biasB = self.sb("p5c_biasB", [128, 8], F32, st)
        w2f_hi = [self.sb(f"p5c_w2fh{i}", [128, 8 * 512], BF16, st) for i in range(2)]
        w2f_lo = [self.sb(f"p5c_w2fl{i}", [128, 8 * 512], BF16, st) for i in range(2)]
        probs = self.sb("p5c_probs", [128, 8], F32, st)
        wtile = self.sb("p5c_wtile", [128, 8], F32, st)
        tt1 = self.sb("p5c_t1", [128, 8], F32, st)
        st8 = self.sb("p5c_st8", [128, 8], F32, st)
        rt_ps = self.psum("p5c_ps", [128, 16], F32, st)
        ldw5 = self.sem("p5c_ldw"); spl5 = self.sem("p5c_spl")
        stg5 = self.sem("p5c_stg"); trl5 = self.sem("p5c_trl")
        ldg = [self.sem(f"p5c_ldg{i}") for i in range(2)]
        ms5 = self.sem("p5c_ms")
        mm5 = self.sem("p5c_mm"); ev5 = self.sem("p5c_ev")
        sg5 = self.sem("p5c_sg")
        so5 = self.sem("p5c_so")
        n_tg = S // 512
        with nc.Block() as blk:
            @blk.sync
            def _(sy):
                sy.wait_ge(ms5, 1)
                sy.dma_start(out=rstage[0:7, :], in_=self.win["router_w"]).then_inc(ldw5, 16)
                sy.dma_start(out=biasB[0:1, 0:7], in_=self.win["routing_bias"]).then_inc(ldw5, 16)
                p = 1
                cnt = 2
                while p < 128:
                    q = min(2 * p, 128)
                    sy.wait_ge(ldw5, 16 * cnt)
                    sy.dma_start(out=biasB[p:q, :], in_=biasB[0:q - p, :]).then_inc(ldw5, 16)
                    cnt += 1
                    p = q
                for g in range(n_tg):
                    b = g % 2
                    if g >= 2:
                        sy.wait_ge(mm5, 4 * (g - 1))
                    for cchunk in range(8):
                        sy.dma_start(out=w2f_hi[b][:, cchunk * 512:(cchunk + 1) * 512],
                                     in_=self.scr["w2t_hi"][g * 512:(g + 1) * 512,
                                                            cchunk * 128:(cchunk + 1) * 128],
                                     transpose=True).then_inc(ldg[b], 16)
                        sy.dma_start(out=w2f_lo[b][:, cchunk * 512:(cchunk + 1) * 512],
                                     in_=self.scr["w2t_lo"][g * 512:(g + 1) * 512,
                                                            cchunk * 128:(cchunk + 1) * 128],
                                     transpose=True).then_inc(ldg[b], 16)

            @blk.vector
            def _(v):
                v.memset(rstage[:], 0.0)
                v.memset(biasB[0:1, 7:8], 0.0)
                v.drain()
                v.nop().then_inc(ms5, 1)
                v.wait_ge(ldw5, 32)
                v.tensor_copy(rsp_hi[:], rstage[:])
                v.drain()
                v.tensor_sub(rsp_lo[:], rstage[:], rsp_hi[:])
                v.drain()
                v.nop().then_inc(spl5, 1)
                v.wait_ge(ldw5, 144)
                iT = 0
                for g in range(n_tg):
                    for ts in range(4):
                        v.wait_ge(mm5, iT + 1)
                        if iT >= 1:
                            v.wait_ge(so5, 16 * iT)
                        v.tensor_tensor(probs[:, 0:7], rt_ps[0:128, 0:7], biasB[:, 0:7], ALU.add)
                        v.drain()
                        v.nop().then_inc(ev5, 1)
                        v.wait_ge(sg5, iT + 1)
                        v.reduce_max(tt1[:, 0:1], probs[:, 0:7], axis=AXX)
                        v.drain()
                        v.tensor_scalar(st8[:, 0:7], probs[:, 0:7], tt1[:, 0:1], None, ALU.is_ge)
                        v.drain()
                        v.scalar_tensor_tensor(st8[:, 0:7], st8[:, 0:7], NEGBIG, probs[:, 0:7],
                                               ALU.mult, ALU.add)
                        v.drain()
                        v.reduce_max(tt1[:, 1:2], st8[:, 0:7], axis=AXX)
                        v.drain()
                        v.tensor_tensor(tt1[:, 2:3], tt1[:, 0:1], tt1[:, 1:2], ALU.add)
                        v.drain()
                        v.reciprocal(tt1[:, 3:4], tt1[:, 2:3])
                        v.drain()
                        v.tensor_scalar(wtile[:, 0:7], probs[:, 0:7], tt1[:, 1:2], tt1[:, 3:4],
                                        ALU.is_ge, ALU.mult)
                        v.drain()
                        v.tensor_mul(wtile[:, 0:7], wtile[:, 0:7], probs[:, 0:7])
                        v.drain()
                        v.memset(wtile[:, 7:8], 0.0)
                        v.drain()
                        v.nop().then_inc(ev5, 1)
                        iT += 1

            @blk.scalar
            def _(sc):
                sc.wait_ge(spl5, 1)
                sc.dma_start(out=self.scr["wsthi"][0:16, 0:C], in_=rsp_hi[:]).then_inc(stg5, 16)
                sc.dma_start(out=self.scr["wstlo"][0:16, 0:C], in_=rsp_lo[:]).then_inc(stg5, 16)
                sc.wait_ge(stg5, 32)
                for cchunk in range(8):
                    sc.dma_start(out=rwT_hi[:, cchunk * 16:(cchunk + 1) * 16],
                                 in_=self.scr["wsthi"][0:16, cchunk * 128:(cchunk + 1) * 128],
                                 transpose=True).then_inc(stg5, 16)
                    sc.dma_start(out=rwT_lo[:, cchunk * 16:(cchunk + 1) * 16],
                                 in_=self.scr["wstlo"][0:16, cchunk * 128:(cchunk + 1) * 128],
                                 transpose=True).then_inc(stg5, 16)
                sc.wait_ge(stg5, 32 + 16 * 16)
                sc.nop().then_inc(trl5, 1)
                iT = 0
                for g in range(n_tg):
                    for ts in range(4):
                        sc.wait_ge(ev5, 2 * iT + 1)
                        sc.activation(probs[:, 0:7], probs[:, 0:7], AF.Sigmoid,
                                      bias=self.consts[:, 1:2], scale=1.0).then_inc(sg5, 1)
                        iT += 1

            @blk.gpsimd
            def _(gp):
                iT = 0
                for g in range(n_tg):
                    for ts in range(4):
                        gp.wait_ge(ev5, 2 * iT + 2)
                        r0 = g * 512 + ts * 128
                        gp.dma_start(out=self.scr["wgt_sc"][r0:r0 + 128, :],
                                     in_=wtile[:]).then_inc(so5, 16)
                        iT += 1
                gp.wait_ge(so5, 16 * 4 * n_tg)

            @blk.tensor
            def _(t):
                t.wait_ge(trl5, 1)
                iT = 0
                for g in range(n_tg):
                    b = g % 2
                    t.wait_ge(ldg[b], 256 * (g // 2 + 1))
                    for ts in range(4):
                        if iT >= 1:
                            t.wait_ge(ev5, 2 * iT - 1)
                        for cchunk in range(8):
                            lh = w2f_hi[b][:, cchunk * 512 + ts * 128: cchunk * 512 + ts * 128 + 128]
                            ll = w2f_lo[b][:, cchunk * 512 + ts * 128: cchunk * 512 + ts * 128 + 128]
                            rh = rwT_hi[:, cchunk * 16:(cchunk + 1) * 16]
                            rl = rwT_lo[:, cchunk * 16:(cchunk + 1) * 16]
                            ins = self.fp32mm(t, rt_ps[:], lh, ll, rh, rl,
                                              start=(cchunk == 0), stop=(cchunk == 7))
                        ins.then_inc(mm5, 1)
                        iT += 1
        self.end_phase()
        st.close()

    def p5d_sort(self):
        nc, S, C = self.nc, self.S, self.C
        n_tt = S // 128
        ER, CE = self.ER, self.CE
        NC = n_tt * ER
        TRASH = float(ER * CE)
        st = ExitStack()
        wgtL = self.sb("p5d_wgtL", [128, n_tt * 8], F32, st)
        Msel = self.sb("p5d_Msel", [128, NC], F32, st)
        Mb = self.sb("p5d_Mb", [128, NC], BF16, st)
        onesT = self.sb("p5d_onesT", [128, 128], BF16, st)
        LtriP = self.sb("p5d_LtriP", [128, 128], BF16, st)
        Ltri16 = self.sb("p5d_Ltri16", [128, 16], BF16, st)
        onesc = self.sb("p5d_onesc", [128, 1], BF16, st)
        rank_sb = self.sb("p5d_rank", [128, NC], F32, st)
        totrow = self.sb("p5d_tot", [1, NC], F32, st)
        t16 = self.sb("p5d_t16", [16, ER], F32, st)
        t16h = self.sb("p5d_t16h", [16, ER], BF16, st)
        t16l = self.sb("p5d_t16l", [16, ER], BF16, st)
        off16 = self.sb("p5d_off16", [16, ER], F32, st)
        offB = self.sb("p5d_offB", [128, NC], F32, st)
        eG = self.sb("p5d_eG", [128, NC], F32, st)
        tmpA = self.sb("p5d_tmpA", [128, NC], F32, st)
        tmpB = self.sb("p5d_tmpB", [128, NC], F32, st)
        posT = self.sb("p5d_posT", [128, NC], F32, st)
        trB = self.sb("p5d_trB", [128, NC], F32, st)
        posTi = self.sb("p5d_posTi", [128, NC], I32, st)
        p12f = self.sb("p5d_p12f", [128, 2 * n_tt], F32, st)
        p12i = self.sb("p5d_p12i", [128, 2 * n_tt], I32, st)
        mx1 = self.sb("p5d_mx1", [128, 2], F32, st)
        w2rh = [self.sb(f"p5d_w2rh{i}", [128, C], BF16, st) for i in range(2)]
        w2rl = [self.sb(f"p5d_w2rl{i}", [128, C], BF16, st) for i in range(2)]
        rk_ps = self.psum("p5d_rkps", [128, NC], F32, st)
        tot_ps = self.psum("p5d_totps", [1, NC], F32, st)
        off_ps = self.psum("p5d_offps", [16, ER], F32, st)
        sLD = self.sem("p5d_sLD"); sZ1 = self.sem("p5d_sZ1")
        sAF = self.sem("p5d_sAF"); sMB = self.sem("p5d_sMB")
        sMM = self.sem("p5d_sMM"); sTV = self.sem("p5d_sTV")
        sT16 = self.sem("p5d_sT16"); sTS = self.sem("p5d_sTS")
        sOM = self.sem("p5d_sOM"); sOV = self.sem("p5d_sOV")
        sOB = self.sem("p5d_sOB"); sPT = self.sem("p5d_sPT")
        sWR = [self.sem(f"p5d_sWR{i}") for i in range(2)]
        sWB = [self.sem(f"p5d_sWB{i}") for i in range(2)]
        sT16a = self.sem("p5d_sT16a"); sOBa = self.sem("p5d_sOBa")
        sTR = self.sem("p5d_sTR")
        tsc = self.dram("p5d_tsc", [16, max(NC, ER)], F32)
        dbl_steps = []
        w = ER
        while w < NC:
            dbl_steps.append(w)
            w = min(2 * w, NC)
        with nc.Block() as blk:
            @blk.sync
            def _(sy):
                sy.dma_start(out=wgtL[:].rearrange("p (a c) -> p a c", c=8),
                             in_=self.scr["wgt_sc"].rearrange("(a p) c -> p a c", p=128)
                             ).then_inc(sLD, 16)
                for a in range(n_tt):
                    b = a % 2
                    if a >= 2:
                        sy.wait_ge(sWB[b], 16 * (ER + 5) * (a // 2))
                    sy.dma_start(out=w2rh[b][:],
                                 in_=self.scr["w2t_hi"][a * 128:(a + 1) * 128, :]).then_inc(sWR[b], 16)
                    sy.dma_start(out=w2rl[b][:],
                                 in_=self.scr["w2t_lo"][a * 128:(a + 1) * 128, :]).then_inc(sWR[b], 16)

            @blk.vector
            def _(v):
                v.memset(onesT[:], 1.0)
                v.memset(onesc[:], 1.0)
                for e in range(ER):
                    v.memset(eG[:, e:e + 1], float(e * CE))
                v.drain()
                v.nop().then_inc(sZ1, 1)
                v.wait_ge(sLD, 16)
                for a in range(n_tt):
                    v.tensor_scalar(Msel[:, a * ER:(a + 1) * ER],
                                    wgtL[:, a * 8:a * 8 + ER], 0.0, None, ALU.is_gt)
                v.drain()
                v.tensor_copy(Mb[:], Msel[:])
                v.drain()
                v.nop().then_inc(sMB, 1)
                v.wait_ge(sMM, 1)
                v.tensor_copy(rank_sb[:], rk_ps[:])
                v.tensor_copy(totrow[:], tot_ps[:])
                v.drain()
                v.nop().then_inc(sTV, 1)
                v.wait_ge(sT16, 16)
                v.tensor_copy(t16h[0:n_tt, :], t16[0:n_tt, :])
                v.drain()
                v.tensor_sub(t16l[0:n_tt, :], t16[0:n_tt, :], t16h[0:n_tt, :])
                v.drain()
                v.nop().then_inc(sTS, 1)
                v.wait_ge(sOM, 1)
                v.tensor_copy(off16[0:n_tt, :], off_ps[0:n_tt, :])
                v.drain()
                v.nop().then_inc(sOV, 1)
                v.wait_ge(sOB, 16 * (8 + len(dbl_steps)))
                v.tensor_add(tmpA[:], rank_sb[:], offB[:])
                v.drain()
                v.tensor_scalar(tmpB[:], tmpA[:], float(CE), None, ALU.is_ge)
                v.drain()
                v.tensor_mul(tmpB[:], tmpB[:], Msel[:])
                v.drain()
                v.tensor_sub(tmpB[:], Msel[:], tmpB[:])
                v.tensor_add(tmpA[:], tmpA[:], eG[:])
                v.drain()
                v.wait_ge(sTR, 1)
                v.tensor_sub(tmpA[:], tmpA[:], trB[:])
                v.drain()
                v.tensor_mul(tmpA[:], tmpA[:], tmpB[:])
                v.drain()
                v.tensor_add(posT[:], tmpA[:], trB[:])
                v.drain()
                v.tensor_copy(posTi[:], posT[:])
                v.drain()
                for a in range(n_tt):
                    cols = slice(a * ER, (a + 1) * ER)
                    v.tensor_scalar(tmpB[:, cols], posT[:, cols], -1.0, None, ALU.mult)
                    v.tensor_mul(tmpA[:, cols], posT[:, cols], Msel[:, cols])
                    v.drain()
                    v.reduce_max(mx1[:, 0:1], tmpB[:, cols], axis=AXX)
                    v.reduce_sum(mx1[:, 1:2], tmpA[:, cols], axis=AXX)
                    v.drain()
                    v.tensor_scalar(p12f[:, 2 * a:2 * a + 1], mx1[:, 0:1], -1.0,
                                    None, ALU.mult)
                    v.drain()
                    v.tensor_tensor(p12f[:, 2 * a + 1:2 * a + 2], mx1[:, 1:2],
                                    p12f[:, 2 * a:2 * a + 1], ALU.subtract)
                    v.drain()
                    v.tensor_copy(p12i[:, 2 * a:2 * a + 2],
                                  p12f[:, 2 * a:2 * a + 2]).then_inc(sPT, 1)

            @blk.gpsimd
            def _(gp):
                gp.iota(trB[:], [[0, 1], [0, 1], [0, 1], [1, NC]], base=int(TRASH),
                        channel_multiplier=0,
                        allow_small_or_imprecise_dtypes=True).then_inc(sTR, 1)
                gp.wait_ge(sZ1, 1)
                gp.affine_select(LtriP[:], onesT[:],
                                 [[0, 1], [0, 1], [0, 1], [1, 128]],
                                 ALU.is_ge, 0.0, base=-1, channel_multiplier=-1)
                ins = gp.affine_select(Ltri16[:], onesT[:, 0:16],
                                       [[0, 1], [0, 1], [0, 1], [1, 16]],
                                       ALU.is_ge, 0.0, base=-1, channel_multiplier=-1)
                ins.then_inc(sAF, 1)
                gp.wait_ge(sTV, 1)
                gp.dma_start(out=tsc[0:1, 0:NC], in_=totrow[0:1, :]).then_inc(sT16a, 16)
                gp.wait_ge(sT16a, 16)
                gp.dma_start(out=t16[0:n_tt, :],
                             in_=tsc[0:1, 0:NC].rearrange("p (a e) -> (p a) e", e=ER)
                             ).then_inc(sT16, 16)
                gp.wait_ge(sOV, 1)
                gp.dma_start(out=tsc[1:2, 0:NC].rearrange("p (a e) -> (p a) e", e=ER),
                             in_=off16[0:n_tt, :]).then_inc(sOBa, 16)
                gp.wait_ge(sOBa, 16)
                gp.dma_start(out=offB[0:1, :], in_=tsc[1:2, 0:NC]).then_inc(sOB, 16)
                cnt = 1
                p = 1
                while p < 128:
                    q = min(2 * p, 128)
                    gp.wait_ge(sOB, 16 * cnt)
                    gp.dma_start(out=offB[p:q, :], in_=offB[0:q - p, :]).then_inc(sOB, 16)
                    cnt += 1
                    p = q
                for i, w in enumerate(dbl_steps):
                    gp.wait_ge(sOB if i == 0 else sOB, 16 * cnt)
                    w2 = min(2 * w, NC)
                    gp.dma_start(out=eG[:, w:w2], in_=eG[:, 0:w2 - w]).then_inc(sOB, 16)
                    cnt += 1
                for a in range(n_tt):
                    b = a % 2
                    gp.wait_ge(sPT, a + 1)
                    gp.wait_ge(sWR[b], 32 * (a // 2 + 1))
                    for e in range(ER):
                        gp.indirect_dma_start(
                            self.scr["wgtS_sc"],
                            IndirectOffsetOnAxis(posTi[:, a * ER + e:a * ER + e + 1], 0),
                            wgtL[:, a * 8 + e:a * 8 + e + 1], None).then_inc(sWB[b], 16)
                    for pcol in range(2):
                        idx = p12i[:, 2 * a + pcol:2 * a + pcol + 1]
                        gp.indirect_dma_start(
                            self.scr["gsc_hi"], IndirectOffsetOnAxis(idx, 0),
                            w2rh[b][:], None).then_inc(sWB[b], 16)
                        gp.indirect_dma_start(
                            self.scr["gsc_lo"], IndirectOffsetOnAxis(idx, 0),
                            w2rl[b][:], None).then_inc(sWB[b], 16)
                    gp.dma_start(out=self.scr["posb"][a * 128:(a + 1) * 128, :],
                                 in_=p12i[:, 2 * a:2 * a + 2]).then_inc(sWB[b], 16)
                for b in range(2):
                    ntl = len([a for a in range(n_tt) if a % 2 == b])
                    gp.wait_ge(sWB[b], 16 * (ER + 5) * ntl)

            @blk.tensor
            def _(t):
                t.wait_ge(sMB, 1)
                t.wait_ge(sAF, 1)
                t.matmul(rk_ps[:], LtriP[:], Mb[:], start=True, stop=True)
                t.matmul(tot_ps[0:1, :], onesc[:], Mb[:], start=True,
                         stop=True).then_inc(sMM, 1)
                t.wait_ge(sTS, 1)
                t.matmul(off_ps[0:n_tt, :], Ltri16[0:n_tt, 0:n_tt], t16h[0:n_tt, :],
                         start=True, stop=False)
                t.matmul(off_ps[0:n_tt, :], Ltri16[0:n_tt, 0:n_tt], t16l[0:n_tt, :],
                         start=False, stop=True).then_inc(sOM, 1)
        self.end_phase()
        st.close()

    def p6_moe(self):
        nc, T, S, C = self.nc, self.T, self.S, self.C
        IC = 256
        n_ic = self.I // IC
        n_tg = S // 512
        st = ExitStack()
        sparse0 = not self.dense
        if not sparse0:
            w2f_hi = self.sb("p6_w2fh", [128, 8 * S], BF16, st)
            w2f_lo = self.sb("p6_w2fl", [128, 8 * S], BF16, st)
            wgtI = self.sb("p6_wgt", [128, (S // 128) * 8], F32, st)
        ldg = self.sem("p6_ldg")
        NW = 2
        gT_hi = [self.sb(f"p6_gTh{i}", [128, 8 * IC], BF16, st) for i in range(NW)]
        gT_lo = [self.sb(f"p6_gTl{i}", [128, 8 * IC], BF16, st) for i in range(NW)]
        uT_hi = [self.sb(f"p6_uTh{i}", [128, 8 * IC], BF16, st) for i in range(NW)]
        uT_lo = [self.sb(f"p6_uTl{i}", [128, 8 * IC], BF16, st) for i in range(NW)]
        dT_hi = [self.sb(f"p6_dTh{i}", [128, 2 * C], BF16, st) for i in range(NW)]
        dT_lo = [self.sb(f"p6_dTl{i}", [128, 2 * C], BF16, st) for i in range(NW)]
        wstage = self.sb("p6_wstage", [128, 2 * C], F32, st)
        wsp_hi = self.sb("p6_wsp_hi", [128, 2 * C], BF16, st)
        wsp_lo = self.sb("p6_wsp_lo", [128, 2 * C], BF16, st)
        hb_hi = [self.sb(f"p6_hbh{i}", [128, 2 * 512], BF16, st) for i in range(2)]
        hb_lo = [self.sb(f"p6_hbl{i}", [128, 2 * 512], BF16, st) for i in range(2)]
        sg_sb = self.sb("p6_sg", [128, 512], F32, st)
        h32 = self.sb("p6_h32", [128, 512], F32, st)
        dstage = [self.sb(f"p6_dst{i}", [128, C], F32, st) for i in range(2)]
        g_ps = self.psum("p6_gps", [128, 512], F32, st)
        u_ps = self.psum("p6_ups", [128, 512], F32, st)
        d_ps = [self.psum(f"p6_dps{i}", [128, 512], F32, st) for i in range(4)]
        ldw = self.sem("p6_ldw")
        ldt6 = self.sem("p6_ldt")
        spl = self.sem("p6_spl")
        stg = self.sem("p6_stg")
        trl = self.sem("p6_trl")
        mmg = self.sem("p6_mmg")
        mmu = self.sem("p6_mmu")
        sSg = self.sem("p6_sSg")
        sH = self.sem("p6_sH")
        sHh = self.sem("p6_sHh")
        sHl = self.sem("p6_sHl")
        mmd = self.sem("p6_mmd")
        sEv = self.sem("p6_sEv")
        sAc2 = [self.sem(f"p6_sAc{i}") for i in range(2)]
        def stgc(pp):
            full, rem = divmod(pp + 1, 3)
            return full * 384 + (0, 64, 128)[rem]
        sparse = not self.dense
        CE = self.CE
        if sparse:
            SHP = max(1, S // CE)
            shtok = min(CE, S)
            EXP = [("shared", i) for i in range(SHP)] +                   [("routed", e) for e in range(self.ER)]
        else:
            EXP = [("shared", None)] + [("routed", e) for e in range(self.ER)]
        NE = len(EXP)
        n_wt = NE * n_ic
        def rng_of(wi):
            if not sparse:
                return n_tg
            return (shtok // 512 or 1) if EXP[wi // n_ic][0] == "shared" else CE // 512
        gpref = [0]
        for _wi in range(n_wt):
            gpref.append(gpref[-1] + rng_of(_wi))
        tpref = [4 * x for x in gpref]
        hpref = [2 * x for x in gpref]
        if sparse:
            NSLOT = self.ER * CE + 128
            a2f_hi = [self.sb(f"p6_a2fh{i}", [128, 8 * CE], BF16, st) for i in range(2)]
            a2f_lo = [self.sb(f"p6_a2fl{i}", [128, 8 * CE], BF16, st) for i in range(2)]
            wgtSI = self.sb("p6_wgtSI", [128, NSLOT // 128], F32, st)
            lda = [self.sem(f"p6_lda{i}") for i in range(2)]
            ldg_total = 16 * ((self.ER * CE + 128) // 128)
        else:
            ldg_total = 16 * (16 * n_tg + 1)

        def wsrc(ei, which):
            nm, e = EXP[ei]
            if nm == "shared":
                return self.win["shared_" + which]
            return self.win["routed_" + which][e]

        with nc.Block() as blk:
            @blk.sync
            def _(sy):
                if not sparse:
                    for g in range(n_tg):
                        for cchunk in range(8):
                            sy.dma_start(out=w2f_hi[:, cchunk * S + g * 512: cchunk * S + (g + 1) * 512],
                                         in_=self.scr["w2t_hi"][g * 512:(g + 1) * 512,
                                                                cchunk * 128:(cchunk + 1) * 128],
                                         transpose=True).then_inc(ldg, 16)
                            sy.dma_start(out=w2f_lo[:, cchunk * S + g * 512: cchunk * S + (g + 1) * 512],
                                         in_=self.scr["w2t_lo"][g * 512:(g + 1) * 512,
                                                                cchunk * 128:(cchunk + 1) * 128],
                                         transpose=True).then_inc(ldg, 16)
                    sy.dma_start(out=wgtI[:].rearrange("p (a c) -> p a c", c=8),
                                 in_=self.scr["wgt_sc"].rearrange("(a p) c -> p a c", p=128)
                                 ).then_inc(ldg, 16)
                else:
                    for a_ in range(NSLOT // 128):
                        sy.dma_start(out=wgtSI[:, a_:a_ + 1],
                                     in_=self.scr["wgtS_sc"][a_ * 128:(a_ + 1) * 128, 0:1]
                                     ).then_inc(ldg, 16)
                def a2f_load(p):
                    eb = p % 2
                    if p >= 2:
                        sy.wait_ge(mmu, hpref[(p - 1) * n_ic])
                    nm, idx = EXP[p]
                    if nm == "shared":
                        src_h, src_l = self.scr["w2t_hi"], self.scr["w2t_lo"]
                        r0, nrow = idx * CE, shtok
                    else:
                        src_h, src_l = self.scr["gsc_hi"], self.scr["gsc_lo"]
                        r0, nrow = idx * CE, CE
                    for cchunk in range(8):
                        sy.dma_start(out=a2f_hi[eb][:, cchunk * CE:cchunk * CE + nrow],
                                     in_=src_h[r0:r0 + nrow,
                                               cchunk * 128:(cchunk + 1) * 128],
                                     transpose=True).then_inc(lda[eb], 16)
                        sy.dma_start(out=a2f_lo[eb][:, cchunk * CE:cchunk * CE + nrow],
                                     in_=src_l[r0:r0 + nrow,
                                               cchunk * 128:(cchunk + 1) * 128],
                                     transpose=True).then_inc(lda[eb], 16)
                if sparse:
                    a2f_load(0)
                    if NE >= 2:
                        a2f_load(1)
                for wi in range(n_wt):
                    ei, ic = divmod(wi, n_ic)
                    if sparse and ic == 1 and 1 <= ei <= NE - 2:
                        a2f_load(ei + 1)
                    if wi >= 1:
                        sy.wait_ge(spl, 3 * wi)
                    for tl in range(2):
                        sy.dma_start(out=wstage[:, tl * C:(tl + 1) * C],
                                     in_=wsrc(ei, "gate")[ic * IC + tl * 128: ic * IC + (tl + 1) * 128, :]
                                     ).then_inc(ldw, 16)
                    sy.wait_ge(spl, 3 * wi + 1)
                    for tl in range(2):
                        sy.dma_start(out=wstage[:, tl * C:(tl + 1) * C],
                                     in_=wsrc(ei, "up")[ic * IC + tl * 128: ic * IC + (tl + 1) * 128, :]
                                     ).then_inc(ldw, 16)
                    sy.wait_ge(spl, 3 * wi + 2)
                    for tl in range(8):
                        sy.dma_start(out=wstage[:, tl * IC:(tl + 1) * IC],
                                     in_=wsrc(ei, "down")[tl * 128:(tl + 1) * 128,
                                                          ic * IC:(ic + 1) * IC]).then_inc(ldw, 16)

            @blk.vector
            def _(v):
                iH = 0
                iEv = 0

                def emit_splits(wi):
                    for part in range(3):
                        pp = 3 * wi + part
                        v.wait_ge(ldw, 16 * (12 * wi + (2, 4, 12)[part]))
                        if pp >= 1:
                            v.wait_ge(stg, stgc(pp - 1))
                        ncols = 2 * C
                        v.tensor_copy(wsp_hi[:, 0:ncols], wstage[:, 0:ncols])
                        v.drain()
                        v.tensor_sub(wsp_lo[:, 0:ncols], wstage[:, 0:ncols], wsp_hi[:, 0:ncols])
                        v.drain()
                        v.nop().then_inc(spl, 1)

                def emit_compute(wi):
                    nonlocal iH, iEv
                    ei, ic = divmod(wi, n_ic)
                    for g in range(rng_of(wi)):
                        gi = gpref[wi] + g
                        hbb = hb_hi[gi % 2], hb_lo[gi % 2]
                        for m in range(2):
                            iH += 1
                            v.wait_ge(sSg, iH)
                            v.wait_ge(mmu, iH)
                            if m == 0 and gi >= 2:
                                v.wait_ge(mmd, 4 * (gi - 1))
                            v.tensor_mul(h32[:], u_ps[:], sg_sb[:])
                            v.drain()
                            v.nop().then_inc(sH, 1)
                            v.wait_ge(sHh, iH)
                            v.tensor_sub(hbb[1][:, m * 512:(m + 1) * 512], h32[:],
                                         hbb[0][:, m * 512:(m + 1) * 512])
                            v.drain()
                            v.nop().then_inc(sHl, 1)
                        for ts in range(4):
                            iEv += 1
                            v.wait_ge(mmd, iEv)
                            if iEv > 2:
                                p_ = iEv % 2
                                v.wait_ge(sAc2[p_], 16 * len([j for j in range(1, iEv - 1) if j % 2 == p_]))
                            db = dstage[iEv % 2]
                            pp0 = 2 * (iEv % 2)
                            tok0 = g * 512 + ts * 128
                            if EXP[ei][0] == "shared":
                                v.tensor_copy(db[:, 0:512], d_ps[pp0][:])
                                v.tensor_copy(db[:, 512:1024], d_ps[pp0 + 1][:])
                            else:
                                e = EXP[ei][1]
                                if sparse:
                                    stile = e * (CE // 128) + (tok0 // 128)
                                    wcol = wgtSI[:, stile:stile + 1]
                                else:
                                    wcol = wgtI[:, (tok0 // 128) * 8 + e:(tok0 // 128) * 8 + e + 1]
                                v.tensor_scalar_mul(db[:, 0:512], d_ps[pp0][:], wcol)
                                v.tensor_scalar_mul(db[:, 512:1024], d_ps[pp0 + 1][:], wcol)
                            v.drain()
                            v.nop().then_inc(sEv, 1)

                for wi in range(n_wt):
                    emit_splits(wi)
                    if wi >= 1:
                        emit_compute(wi - 1)
                emit_compute(n_wt - 1)

            @blk.gpsimd
            def _(gp):
                iEv = 0
                ntile = n_tg * 4
                for wi in range(n_wt + 1):
                    if wi < n_wt:
                        for part in range(3):
                            pp = 3 * wi + part
                            vh = self.scr["wsthi"] if pp % 2 == 0 else self.scr["wsthi2"]
                            vl = self.scr["wstlo"] if pp % 2 == 0 else self.scr["wstlo2"]
                            gp.wait_ge(spl, pp + 1)
                            if pp >= 2:
                                gp.wait_ge(trl, pp - 1)
                            if part < 2:
                                for tl in range(2):
                                    gp.dma_start(out=vh[tl * 128:(tl + 1) * 128, 0:C],
                                                 in_=wsp_hi[:, tl * C:(tl + 1) * C]).then_inc(stg, 16)
                                    gp.dma_start(out=vl[tl * 128:(tl + 1) * 128, 0:C],
                                                 in_=wsp_lo[:, tl * C:(tl + 1) * C]).then_inc(stg, 16)
                            else:
                                for tl in range(8):
                                    gp.dma_start(out=vh[tl * 128:(tl + 1) * 128, 0:IC],
                                                 in_=wsp_hi[:, tl * IC:(tl + 1) * IC]).then_inc(stg, 16)
                                    gp.dma_start(out=vl[tl * 128:(tl + 1) * 128, 0:IC],
                                                 in_=wsp_lo[:, tl * IC:(tl + 1) * IC]).then_inc(stg, 16)
                    if wi >= 1:
                        pw = wi - 1
                        ei, ic = divmod(pw, n_ic)
                        for g in range(rng_of(pw)):
                            for ts in range(4):
                                iEv += 1
                                gp.wait_ge(sEv, iEv)
                                if iEv > 2:
                                    p_ = iEv % 2
                                    gp.wait_ge(sAc2[p_], 16 * len([j for j in range(1, iEv - 1) if j % 2 == p_]))
                                db = dstage[iEv % 2]
                                tok0 = g * 512 + ts * 128
                                if sparse:
                                    if EXP[ei][0] == "shared":
                                        t0_ = EXP[ei][1] * CE + tok0
                                        dst = self.scr["moe_acc"][t0_:t0_ + 128, :]
                                    else:
                                        s0 = EXP[ei][1] * CE + tok0
                                        dst = self.scr["resbuf"][s0:s0 + 128, :]
                                    gp.dma_start(out=dst, in_=db[:],
                                                 accum_op=(ALU.bypass if ic == 0 else ALU.add)
                                                 ).then_inc(sAc2[iEv % 2], 16)
                                else:
                                    first = (pw == 0)
                                    gp.dma_start(out=self.scr["moe_acc"][tok0:tok0 + 128, :],
                                                 in_=db[:],
                                                 accum_op=(ALU.bypass if first else ALU.add)
                                                 ).then_inc(sAc2[iEv % 2], 16)
                for p_ in range(2):
                    gp.wait_ge(sAc2[p_], 16 * len([j for j in range(1, tpref[n_wt] + 1) if j % 2 == p_]))

            @blk.scalar
            def _(sc):
                iH = 0
                ldt6_n = [0]

                def emit_loads(wi):
                    for part, (thi, tlo) in enumerate(((gT_hi[wi % NW], gT_lo[wi % NW]),
                                                       (uT_hi[wi % NW], uT_lo[wi % NW]),
                                                       (dT_hi[wi % NW], dT_lo[wi % NW]))):
                        pp = 3 * wi + part
                        sc.wait_ge(stg, stgc(pp))
                        if wi >= NW:
                            sc.wait_ge(mmd, tpref[wi - NW + 1])
                        vh = self.scr["wsthi"] if pp % 2 == 0 else self.scr["wsthi2"]
                        vl = self.scr["wstlo"] if pp % 2 == 0 else self.scr["wstlo2"]
                        if part < 2:
                            for cchunk in range(8):
                                sc.dma_start(out=thi[:, cchunk * IC:(cchunk + 1) * IC],
                                             in_=vh[0:IC, cchunk * 128:(cchunk + 1) * 128],
                                             transpose=True).then_inc(ldt6, 16)
                                sc.dma_start(out=tlo[:, cchunk * IC:(cchunk + 1) * IC],
                                             in_=vl[0:IC, cchunk * 128:(cchunk + 1) * 128],
                                             transpose=True).then_inc(ldt6, 16)
                            ldt6_n[0] += 256
                        else:
                            for cchunk in range(2):
                                sc.dma_start(out=thi[:, cchunk * C:(cchunk + 1) * C],
                                             in_=vh[0:1024, cchunk * 128:(cchunk + 1) * 128],
                                             transpose=True).then_inc(ldt6, 16)
                                sc.dma_start(out=tlo[:, cchunk * C:(cchunk + 1) * C],
                                             in_=vl[0:1024, cchunk * 128:(cchunk + 1) * 128],
                                             transpose=True).then_inc(ldt6, 16)
                            ldt6_n[0] += 64
                        sc.wait_ge(ldt6, ldt6_n[0])
                        sc.nop().then_inc(trl, 1)

                def emit_compute(wi):
                    nonlocal iH
                    for g in range(rng_of(wi)):
                        gi = gpref[wi] + g
                        hbb_hi = hb_hi[gi % 2]
                        for m in range(2):
                            iH += 1
                            sc.wait_ge(mmg, iH)
                            if m == 0 and gi >= 2:
                                sc.wait_ge(mmd, 4 * (gi - 1))
                            sc.activation(sg_sb[:], g_ps[:], AF.Silu,
                                          bias=self.consts[:, 1:2], scale=1.0).then_inc(sSg, 1)
                            sc.wait_ge(sH, iH)
                            sc.copy(hbb_hi[:, m * 512:(m + 1) * 512], h32[:]).then_inc(sHh, 1)

                for wi in range(n_wt):
                    emit_loads(wi)
                    if wi >= 1:
                        emit_compute(wi - 1)
                emit_compute(n_wt - 1)

            @blk.tensor
            def _(t):
                t.wait_ge(ldg, ldg_total)
                iH = 0
                iD = 0
                for wi in range(n_wt):
                    ei, ic = divmod(wi, n_ic)
                    wb = wi % NW
                    t.wait_ge(trl, 3 * wi + 3)
                    if sparse and ic == 0:
                        t.wait_ge(lda[ei % 2], 256 * (ei // 2 + 1))
                    for g in range(rng_of(wi)):
                        for m in range(2):
                            iH += 1
                            if iH >= 2:
                                t.wait_ge(sSg, iH - 1)
                            for cchunk in range(8):
                                lh = gT_hi[wb][:, cchunk * IC + m * 128: cchunk * IC + m * 128 + 128]
                                ll = gT_lo[wb][:, cchunk * IC + m * 128: cchunk * IC + m * 128 + 128]
                                if sparse:
                                    eb = ei % 2
                                    rh = a2f_hi[eb][:, cchunk * CE + g * 512: cchunk * CE + (g + 1) * 512]
                                    rl = a2f_lo[eb][:, cchunk * CE + g * 512: cchunk * CE + (g + 1) * 512]
                                else:
                                    rh = w2f_hi[:, cchunk * S + g * 512: cchunk * S + (g + 1) * 512]
                                    rl = w2f_lo[:, cchunk * S + g * 512: cchunk * S + (g + 1) * 512]
                                self.fp32mm(t, g_ps[:], lh, ll, rh, rl,
                                            start=(cchunk == 0), stop=(cchunk == 7),
                                            inc=mmg if cchunk == 7 else None)
                            if iH >= 2:
                                t.wait_ge(sH, iH - 1)
                            for cchunk in range(8):
                                lh = uT_hi[wb][:, cchunk * IC + m * 128: cchunk * IC + m * 128 + 128]
                                ll = uT_lo[wb][:, cchunk * IC + m * 128: cchunk * IC + m * 128 + 128]
                                if sparse:
                                    eb = ei % 2
                                    rh = a2f_hi[eb][:, cchunk * CE + g * 512: cchunk * CE + (g + 1) * 512]
                                    rl = a2f_lo[eb][:, cchunk * CE + g * 512: cchunk * CE + (g + 1) * 512]
                                else:
                                    rh = w2f_hi[:, cchunk * S + g * 512: cchunk * S + (g + 1) * 512]
                                    rl = w2f_lo[:, cchunk * S + g * 512: cchunk * S + (g + 1) * 512]
                                self.fp32mm(t, u_ps[:], lh, ll, rh, rl,
                                            start=(cchunk == 0), stop=(cchunk == 7),
                                            inc=mmu if cchunk == 7 else None)
                        t.wait_ge(sHl, iH)
                        gi_ = gpref[wi] + g
                        hbbh, hbbl = hb_hi[gi_ % 2], hb_lo[gi_ % 2]
                        for ts in range(4):
                            iD += 1
                            if iD > 2:
                                t.wait_ge(sEv, iD - 2)
                            pq0 = 2 * (iD % 2)
                            for half, dps in ((0, d_ps[pq0]), (1, d_ps[pq0 + 1])):
                                for m in range(2):
                                    lh = hbbh[:, m * 512 + ts * 128: m * 512 + ts * 128 + 128]
                                    ll = hbbl[:, m * 512 + ts * 128: m * 512 + ts * 128 + 128]
                                    rh = dT_hi[wb][:, m * C + half * 512: m * C + half * 512 + 512]
                                    rl = dT_lo[wb][:, m * C + half * 512: m * C + half * 512 + 512]
                                    self.fp32mm(t, dps[:], lh, ll, rh, rl,
                                                start=(m == 0), stop=(m == 1),
                                                inc=mmd if (m == 1 and half == 1) else None)
        self.end_phase()
        st.close()

    def p7_final(self):
        nc, S, C = self.nc, self.S, self.C
        n_tt = S // 128
        sparse = not self.dense
        st = ExitStack()
        NB = 2
        s1 = [self.sb(f"p7_s1{i}", [128, C], F32, st) for i in range(NB)]
        ma = [self.sb(f"p7_ma{i}", [128, C], F32, st) for i in range(NB)]
        ot = [self.sb(f"p7_ot{i}", [128, C], F32, st) for i in range(NB)]
        if sparse:
            r1 = [self.sb(f"p7_r1{i}", [128, C], F32, st) for i in range(NB)]
            r2 = [self.sb(f"p7_r2{i}", [128, C], F32, st) for i in range(NB)]
            pix = [self.sb(f"p7_pix{i}", [128, 2], I32, st) for i in range(NB)]
            gt7 = [self.sem(f"p7_gt{i}") for i in range(NB)]
        ld7 = [self.sem(f"p7_ld{i}") for i in range(NB)]
        ev7 = self.sem("p7_ev")
        so7 = [self.sem(f"p7_so{i}") for i in range(NB)]
        with nc.Block() as blk:
            @blk.sync
            def _(sy):
                for i in range(n_tt):
                    b = i % NB
                    if i >= NB:
                        sy.wait_ge(ev7, i - NB + 1)
                    sy.dma_start(out=s1[b][:],
                                 in_=self.scr["s1_sc"][i * 128:(i + 1) * 128, :]).then_inc(ld7[b], 16)
                    sy.dma_start(out=ma[b][:],
                                 in_=self.scr["moe_acc"][i * 128:(i + 1) * 128, :]).then_inc(ld7[b], 16)
                    if sparse:
                        sy.dma_start(out=pix[b][:],
                                     in_=self.scr["posb"][i * 128:(i + 1) * 128, :]).then_inc(ld7[b], 16)

            if sparse:
                @blk.gpsimd
                def _(gp):
                    for i in range(n_tt):
                        b = i % NB
                        if i >= NB:
                            gp.wait_ge(ev7, i - NB + 1)
                        gp.wait_ge(ld7[b], 48 * (i // NB + 1))
                        gp.indirect_dma_start(
                            r1[b][:], None, self.scr["resbuf"],
                            IndirectOffsetOnAxis(pix[b][:, 0:1], 0)).then_inc(gt7[b], 16)
                        gp.indirect_dma_start(
                            r2[b][:], None, self.scr["resbuf"],
                            IndirectOffsetOnAxis(pix[b][:, 1:2], 0)).then_inc(gt7[b], 16)

                @blk.vector
                def _(v):
                    for i in range(n_tt):
                        b = i % NB
                        if i >= NB:
                            v.wait_ge(so7[b], 16 * (i // NB))
                        v.wait_ge(ld7[b], 48 * (i // NB + 1))
                        v.wait_ge(gt7[b], 32 * (i // NB + 1))
                        v.tensor_add(ot[b][:], s1[b][:], ma[b][:])
                        v.drain()
                        v.tensor_add(ot[b][:], ot[b][:], r1[b][:])
                        v.drain()
                        v.tensor_add(ot[b][:], ot[b][:], r2[b][:])
                        v.drain()
                        v.nop().then_inc(ev7, 1)

                @blk.scalar
                def _(sc):
                    for i in range(n_tt):
                        b = i % NB
                        sc.wait_ge(ev7, i + 1)
                        sc.dma_start(out=self.out[i * 128:(i + 1) * 128, :],
                                     in_=ot[b][:]).then_inc(so7[b], 16)
                    for b in range(NB):
                        nfull = len([i for i in range(n_tt) if i % NB == b])
                        sc.wait_ge(so7[b], 16 * nfull)
            else:
                @blk.vector
                def _(v):
                    for i in range(n_tt):
                        b = i % NB
                        if i >= NB:
                            v.wait_ge(so7[b], 16 * (i // NB))
                        v.wait_ge(ld7[b], 32 * (i // NB + 1))
                        v.tensor_add(ot[b][:], s1[b][:], ma[b][:])
                        v.drain()
                        v.nop().then_inc(ev7, 1)

                @blk.gpsimd
                def _(gp):
                    for i in range(n_tt):
                        b = i % NB
                        gp.wait_ge(ev7, i + 1)
                        gp.dma_start(out=self.out[i * 128:(i + 1) * 128, :],
                                     in_=ot[b][:]).then_inc(so7[b], 16)
                    for b in range(NB):
                        nfull = len([i for i in range(n_tt) if i % NB == b])
                        gp.wait_ge(so7[b], 16 * nfull)
        self.end_phase()
        st.close()

    def p_dbg(self):
        dbg_sb = getattr(self, "dbg_sb", {})
        if not self.dbg and not dbg_sb:
            return
        nc = self.nc
        sem = self.sem("dbg_st")
        cnt = 0
        with nc.Block() as blk:
            @blk.gpsimd
            def _(g):
                nonlocal cnt
                for name in self.dbg:
                    ap = self.scr[name]
                    shp = list(ap.shape)
                    dbg_ap = self.nc.dram_tensor("dbg_" + name, shp, ap.dtype,
                                                 kind="ExternalOutput").ap()
                    g.dma_start(out=dbg_ap, in_=ap).then_inc(sem, 16)
                    cnt += 1
                for name, t in dbg_sb.items():
                    shp = list(t[:].shape)
                    dbg_ap = self.nc.dram_tensor("dbg_" + name, shp, t[:].dtype,
                                                 kind="ExternalOutput").ap()
                    g.dma_start(out=dbg_ap, in_=t[:]).then_inc(sem, 16)
                    cnt += 1
                g.wait_ge(sem, 16 * cnt)


_CACHED = {}


def _build():
    if "nc" not in _CACHED:
        m = MK(T=4096)
        _CACHED["nc"] = m.build()
    return _CACHED["nc"]


def kernel(**inputs):
    from concourse.bass_utils import run_bass_kernel_spmd
    nc = _build()
    x = np.asarray(inputs["L0_attn_x"])
    B = x.shape[0]
    wmap = {
        "win_w_q": inputs["L0_attn_w_q"], "win_w_k": inputs["L0_attn_w_k"],
        "win_w_v": inputs["L0_attn_w_v"], "win_w_o": inputs["L0_attn_w_o"],
        "win_ln1_w": np.asarray(inputs["L0_ffn_ln1_w"]).reshape(1, -1),
        "win_ln2_w": np.asarray(inputs["L0_ffn_ln2_w"]).reshape(1, -1),
        "win_kv_d": inputs["L0_ffn_kv_d"], "win_q_d": inputs["L0_ffn_q_d"],
        "win_k_u": inputs["L0_ffn_k_u"], "win_q_u": inputs["L0_ffn_q_u"],
        "win_v_u": inputs["L0_ffn_v_u"],
        "win_rope_q_w": inputs["L0_ffn_rope_q_w"], "win_rope_k_w": inputs["L0_ffn_rope_k_w"],
        "win_o_w": inputs["L0_ffn_o_w"],
        "win_router_w": inputs["L0_ffn_router_w"],
        "win_routing_bias": np.asarray(inputs["L0_ffn_routing_bias"]).reshape(1, -1),
        "win_shared_gate": np.asarray(inputs["L0_ffn_shared_gate"])[0],
        "win_shared_up": np.asarray(inputs["L0_ffn_shared_up"])[0],
        "win_shared_down": np.asarray(inputs["L0_ffn_shared_down"])[0],
        "win_routed_gate": inputs["L0_ffn_routed_gate"],
        "win_routed_up": inputs["L0_ffn_routed_up"],
        "win_routed_down": inputs["L0_ffn_routed_down"],
    }
    wmap = {k: np.ascontiguousarray(np.asarray(v), dtype=np.float32) for k, v in wmap.items()}
    in_maps = []
    for c in range(8):
        b, role = c // 2, c % 2
        im = dict(wmap)
        im["xin"] = np.ascontiguousarray(x[b], dtype=np.float32)
        im["role_in"] = np.array([[role]], dtype=np.int32)
        im["rolef_in"] = np.array([[1.0 - role, float(role)]], dtype=np.float32)
        in_maps.append(im)
    res = run_bass_kernel_spmd(nc, in_maps, core_ids=list(range(8)))
    S = 2048
    out = np.empty((B, 4096, 1024), dtype=np.float32)
    for c in range(8):
        b, role = c // 2, c % 2
        out[b, role * S:(role + 1) * S, :] = np.asarray(res.results[c]["out"])
    return out
